# version 02
# baseline (speedup 1.0000x reference)
import numpy as np
import concourse.bass as bass
import concourse.mybir as mybir
from concourse.bass_utils import run_bass_kernel_spmd

F32 = mybir.dt.float32
I32 = mybir.dt.int32
BF16 = mybir.dt.bfloat16
AF = mybir.ActivationFunctionType
OP = mybir.AluOpType

NCORES = 8
D = 2048
TOK = 1024
DT = 16
NST = 16
E = 16
NU = 18
EPS = 1e-5
MAGIC = 0x5A5A1234
CONST_VALS = (EPS, 1.0, -0.5, -1.0, 1.0 / 2048.0, 0.5, 0.0, 2.0)

ENGINES = ("tensor", "vector", "scalar", "gpsimd", "sync")
SEM_OWNER = {"mm": "tensor", "vec": "vector", "sca": "scalar", "gp": "gpsimd",
             "syn": None, "gpd": None, "sx": None, "scst": None,
             "wg0": None, "wg1": None, "wg2": None, "wg3": None,
             "wu0": None, "wu1": None, "wu2": None, "wu3": None,
             "wd0": None, "wd1": None, "wd2": None, "wd3": None,
             "bb0": None, "bb1": None, "cb0": None, "cb1": None,
             "poll": None}


class Em:

    def __init__(self, nc):
        self.nc = nc
        self.q = {e: [] for e in ENGINES}
        self.sem = {}
        self.cnt = {}
        self.last_wait = {e: {} for e in ENGINES}
        for name in SEM_OWNER:
            self.sem[name] = nc.alloc_semaphore("s_" + name)
            self.cnt[name] = 0

    def emit(self, eng, fn):
        self.q[eng].append(fn)

    def wge(self, eng, name, tgt):
        if tgt <= 0 or (eng == "tensor" and name == "mm"):
            return
        lw = self.last_wait[eng]
        if lw.get(name, 0) >= tgt:
            return
        lw[name] = tgt
        sem = self.sem[name]
        self.emit(eng, lambda E_: E_.wait_ge(sem, tgt))

    def wge_multi(self, eng, waits):
        for name, tgt in waits:
            self.wge(eng, name, tgt)

    def inc(self, eng, name, build, n=1):
        self.cnt[name] += n
        sem = self.sem[name]
        self.emit(eng, lambda E_: build(E_).then_inc(sem, n))
        return self.cnt[name]

    def run(self, eng, build):
        self.emit(eng, lambda E_: build(E_))

    def flush(self, name=None):
        with self.nc.Block(name) as b:
            for eng in ENGINES:
                if self.q[eng]:
                    fns = self.q[eng]
                    self.q[eng] = []
                    getattr(b, eng)(lambda E_, fns=fns: [f(E_) for f in fns])


def build_program(phases=("ffn1", "mamba", "cc", "ffn2", "X")):
    nc = bass.Bass("TRN2", target_bir_lowering=False)

    d_x = nc.dram_tensor("xt", [128, DT * TOK], F32, kind="ExternalInput").ap()
    d_gw = nc.dram_tensor("gw", [NU, 8, 128, 16 * 128], F32, kind="ExternalInput").ap()
    d_uw = nc.dram_tensor("uw", [NU, 8, 128, 16 * 128], F32, kind="ExternalInput").ap()
    d_dw = nc.dram_tensor("dw", [NU, 16, 128, 8 * 128], F32, kind="ExternalInput").ap()
    d_rw = nc.dram_tensor("rw", [16, 128, E], F32, kind="ExternalInput").ap()
    d_xpw = nc.dram_tensor("xpw", [16, 128, 160], F32, kind="ExternalInput").ap()
    d_dtw = nc.dram_tensor("dtw", [128, D], F32, kind="ExternalInput").ap()
    d_alog = nc.dram_tensor("alog", [128, DT * NST], F32, kind="ExternalInput").ap()
    d_dvec = nc.dram_tensor("dvec", [128, DT], F32, kind="ExternalInput").ap()
    d_dtb = nc.dram_tensor("dtb", [128, DT], F32, kind="ExternalInput").ap()
    d_flagE = nc.dram_tensor("flagE", [128, 1], F32, kind="ExternalInput").ap()
    d_flagO = nc.dram_tensor("flagO", [128, 1], F32, kind="ExternalInput").ap()
    d_consts = nc.dram_tensor("consts", [128, 8], F32, kind="ExternalInput").ap()
    d_magic = nc.dram_tensor("magic", [1, 16], I32, kind="ExternalInput").ap()
    d_out = nc.dram_tensor("out", [128, DT * TOK], F32, kind="ExternalOutput").ap()

    bc_dram = nc.dram_tensor("bc_scratch", [32, TOK], BF16)
    sh_h = nc.dram_tensor("sh_h", [2, 128, DT * NST], F32, addr_space="Shared")
    sh_f = nc.dram_tensor("sh_f", [2, 1, 16], I32, addr_space="Shared")

    AB = 16512
    nc.alloc_sbuf_tensor("alias_backing", [128, 14336], F32)
    at = nc.alloc_sbuf_tensor_at
    gw_ring = at("gw_ring", [128, 4 * 2048], BF16, offset=AB + 0)
    uw_ring = at("uw_ring", [128, 4 * 2048], BF16, offset=AB + 16384)
    dw_ring = at("dw_ring", [128, 4 * 1024], BF16, offset=AB + 32768)
    hp = at("hp", [128, 8 * TOK], BF16, offset=AB + 40960)
    dA0 = at("dA0", [128, TOK], F32, offset=AB + 0)
    dA1 = at("dA1", [128, TOK], F32, offset=AB + 4096)
    bbc0 = at("bbc0", [128, TOK], BF16, offset=AB + 8192)
    bbc1 = at("bbc1", [128, TOK], BF16, offset=AB + 10240)
    cbc0 = at("cbc0", [128, TOK], BF16, offset=AB + 12288)
    cbc1 = at("cbc1", [128, TOK], BF16, offset=AB + 14336)
    delta_sb = at("delta_sb", [128, TOK], F32, offset=AB + 16384)
    du_sb = at("du_sb", [128, TOK], BF16, offset=AB + 20480)
    h_sb = at("h_sb", [128, TOK], BF16, offset=AB + 22528)
    y_sb = at("y_sb", [128, TOK], BF16, offset=AB + 24576)
    dbu0 = at("dbu0", [128, TOK], BF16, offset=AB + 26624)
    dbu1 = at("dbu1", [128, TOK], BF16, offset=AB + 28672)
    tmp_m = at("tmp_m", [128, TOK], F32, offset=AB + 30720)
    tmp_bf = at("tmp_bf", [128, TOK], BF16, offset=AB + 34816)
    e_m = at("e_m", [128, 512], F32, offset=AB + 36864)
    l1p_m = at("l1p_m", [128, 512], F32, offset=AB + 38912)
    dAb = (dA0, dA1)
    bbcb = (bbc0, bbc1)
    cbcb = (cbc0, cbc1)
    dbub = (dbu0, dbu1)

    al = nc.alloc_sbuf_tensor
    X = al("X", [128, DT * TOK], BF16)
    XN = al("XN", [128, DT * TOK], BF16)
    xpw_sb = al("xpw_sb", [128, 16 * 160], BF16)
    dtw_sb = al("dtw_sb", [128, D], BF16)
    rw_sb = al("rw_sb", [128, 16 * E], BF16)
    alog_sb = al("alog_sb", [128, DT * NST], F32)
    A_sb = al("A_sb", [128, DT * NST], F32)
    dvec_sb = al("dvec_sb", [128, DT], F32)
    ndtb_sb = al("ndtb_sb", [128, DT], F32)
    dtb_sb = al("dtb_sb", [128, DT], F32)
    consts_sb = al("consts_sb", [128, 8], F32)
    for _i, _v in enumerate(CONST_VALS):
        nc.const_aps.register(F32, _v, consts_sb[:, _i:_i + 1])
    flagE_sb = al("flagE_sb", [128, 1], F32)
    flagO_sb = al("flagO_sb", [128, 1], F32)
    ident = al("ident", [128, 128], BF16)
    ones_col = al("ones_col", [128, 1], BF16)
    ones_row = al("ones_row", [1, 128], BF16)
    wbc_half = al("wbc_half", [128, TOK], BF16)
    sl_b = (al("sl0", [128, 512], F32), al("sl1", [128, 512], F32))
    t1_b = (al("t10", [128, 512], F32), al("t11", [128, 512], F32))
    wbc_b = (al("wbc0", [128, TOK], BF16), al("wbc1", [128, TOK], BF16))
    wrow_b = (al("wrow0", [1, TOK], BF16), al("wrow1", [1, TOK], BF16))
    W16 = al("W16", [128, 8 * E], BF16)
    m8_sb = al("m8", [128, 8], F32)
    nm1_sb = al("nm1", [128, 1], F32)
    e2a_sb = al("e2a", [128, 1], F32)
    e2v_sb = al("e2v", [128, 1], F32)
    fac_sb = al("fac", [128, 1], F32)
    msk_sb = al("msk", [128, E], F32)
    sexp_sb = al("sexp", [128, E], F32)
    lg_sb = al("lg_sb", [128, E], F32)
    ln_t = (al("lnt0", [128, TOK], F32), al("lnt1", [128, TOK], F32))
    sq_t = (al("sqt0", [128, TOK], BF16), al("sqt1", [128, TOK], BF16))
    r_mean = al("r_mean", [1, TOK], F32)
    r_ex2 = al("r_ex2", [1, TOK], F32)
    r_tmp = al("r_tmp", [1, TOK], F32)
    r_var = al("r_var", [1, TOK], F32)
    r_rstd = al("r_rstd", [1, TOK], F32)
    r_mr = al("r_mr", [1, TOK], F32)
    r_rstd_bf = al("r_rstd_bf", [1, TOK], BF16)
    r_mr_bf = al("r_mr_bf", [1, TOK], BF16)
    mg_sb = al("mg_sb", [1, 16], I32)
    zf_sb = al("zf_sb", [1, 16], I32)
    pf_sb = al("pf_sb", [1, 16], I32)
    delta_rT = al("delta_rT", [128, TOK], BF16)
    bc_sb = al("bc_sb", [32, TOK], F32)
    hfinA = al("hfinA", [128, DT * NST], F32)
    hm_sb = al("hm_sb", [128, DT * NST], F32)
    hmz_sb = al("hmz_sb", [128, DT * NST], F32)

    assert nc.sbuf_bytes_remaining >= 0, "SBUF over budget"

    ps = [nc.alloc_psum_tensor(f"ps{i}", [128, 512], F32) for i in range(6)]
    psG = nc.alloc_psum_tensor("psG", [128, 512], F32)
    psH = nc.alloc_psum_tensor("psH", [128, TOK], BF16)

    em = Em(nc)
    bank_free = {i: [] for i in range(6)}
    bank_free["G"] = []
    bank_free["H"] = []
    state = {"consts_gp": 0, "x_gpd": 0, "ph0_syn": 0, "ph0_gpd": 0, "a_sca": 0}

    def mm_group(bank, builds, extra_waits=()):
        em.wge_multi("tensor", bank_free[bank])
        em.wge_multi("tensor", extra_waits)
        done = None
        for i, b in enumerate(builds):
            if i == len(builds) - 1:
                done = em.inc("tensor", "mm", b)
            else:
                em.run("tensor", b)
        return done

    def ph0():
        for k in range(4):
            em.inc("gpsimd", "sx", lambda E_, k=k: E_.dma_start(
                X[:, k * 4096:(k + 1) * 4096], d_x[:, k * 4096:(k + 1) * 4096]), 16)
        state["x_gpd"] = em.cnt["sx"]
        em.inc("gpsimd", "scst", lambda E_: E_.dma_start(
            xpw_sb[:, :].rearrange("p (a c) -> p a c", a=16),
            d_xpw.rearrange("a b c -> b a c")), 16)
        em.inc("gpsimd", "scst", lambda E_: E_.dma_start(dtw_sb[:, :], d_dtw), 16)
        em.inc("gpsimd", "scst", lambda E_: E_.dma_start(
            rw_sb[:, :].rearrange("p (a c) -> p a c", a=16),
            d_rw.rearrange("a b c -> b a c")), 16)
        em.inc("gpsimd", "scst", lambda E_: E_.dma_start(mg_sb[:, :], d_magic), 16)
        state["ph0_gpd"] = em.cnt["scst"]
        em.inc("sync", "syn", lambda E_: E_.dma_start(alog_sb[:, :], d_alog), 16)
        em.inc("sync", "syn", lambda E_: E_.dma_start(dvec_sb[:, :], d_dvec), 16)
        em.inc("sync", "syn", lambda E_: E_.dma_start(dtb_sb[:, :], d_dtb), 16)
        em.inc("sync", "syn", lambda E_: E_.dma_start(flagE_sb[:, :], d_flagE), 16)
        em.inc("sync", "syn", lambda E_: E_.dma_start(consts_sb[:, :], d_consts), 16)
        syn_c = em.inc("sync", "syn", lambda E_: E_.dma_start(flagO_sb[:, :], d_flagO), 16)
        state["ph0_syn"] = syn_c
        zc = em.inc("gpsimd", "gp", lambda E_: E_.memset(zf_sb[:, :], 0))
        em.wge("gpsimd", "gp", zc)

        _gpd_sem = em.sem["gpd"]

        def _zero_flag(g):
            for arm in g.switch_core_id(n=128):
                c = int(arm.logical)
                g.dma_start(sh_f.ap()[c % 2], zf_sb[:, :]).then_inc(_gpd_sem, 16)
        em.cnt["gpd"] += 16
        em.run("gpsimd", _zero_flag)
        c = em.inc("gpsimd", "gp", lambda E_: E_.memset(ident[:, :], 0.0))
        em.wge("gpsimd", "gp", c)
        em.inc("gpsimd", "gp", lambda E_: E_.affine_select(
            out=ident[:, :], in_=ident[:, :], compare_op=OP.not_equal,
            fill=1.0, base=0, pattern=[[-1, 128]], channel_multiplier=1))
        em.run("gpsimd", lambda E_: E_.memset(ones_col[:, :], 1.0))
        em.run("gpsimd", lambda E_: E_.memset(ones_row[:, :], 1.0))
        state["consts_gp"] = em.inc("gpsimd", "gp", lambda E_: E_.memset(wbc_half[:, :], 0.5))
        em.wge("scalar", "syn", syn_c)
        c = em.inc("scalar", "sca", lambda E_: E_.activation(A_sb[:, :], alog_sb[:, :], AF.Exp))
        em.wge("scalar", "sca", c)
        em.inc("scalar", "sca", lambda E_: E_.mul(A_sb[:, :], A_sb[:, :], -1.0))
        state["a_sca"] = em.inc("scalar", "sca", lambda E_: E_.mul(ndtb_sb[:, :], dtb_sb[:, :], -1.0))

    def layernorm(x_ready_vec):
        em.wge_multi("tensor", bank_free[0] + bank_free[1] + bank_free[2] + bank_free[3])
        em.wge_multi("tensor", [("sx", state["x_gpd"]), ("gp", state["consts_gp"]),
                                ("vec", x_ready_vec)])
        sq_war = {}
        s2_done = None
        for dt in range(DT):
            par = dt % 2
            em.wge("scalar", "mm", sq_war.get(par, 0))
            em.wge("scalar", "sx", state["x_gpd"])
            em.wge("scalar", "vec", x_ready_vec)
            sq_c = em.inc("scalar", "sca", lambda E_, dt=dt, par=par: E_.activation(
                sq_t[par][:, :], X[:, dt * TOK:(dt + 1) * TOK], AF.Square))
            for tn in range(2):
                em.run("tensor", lambda E_, dt=dt, tn=tn: E_.matmul(
                    ps[tn][0:1, :], ones_col[:, :],
                    X[:, dt * TOK + tn * 512: dt * TOK + (tn + 1) * 512],
                    start=(dt == 0), stop=(dt == DT - 1)))
            em.wge("tensor", "sca", sq_c)
            for tn in range(2):
                b = lambda E_, dt=dt, tn=tn, par=par: E_.matmul(
                    ps[2 + tn][0:1, :], ones_col[:, :],
                    sq_t[par][:, tn * 512:(tn + 1) * 512],
                    start=(dt == 0), stop=(dt == DT - 1))
                if tn == 1:
                    s2_done = em.inc("tensor", "mm", b)
                    sq_war[par] = s2_done
                else:
                    em.run("tensor", b)
        em.wge("scalar", "mm", s2_done)
        em.inc("scalar", "sca", lambda E_: E_.mul(r_mean[0:1, 0:512], ps[0][0:1, :], 1.0 / D))
        c2 = em.inc("scalar", "sca", lambda E_: E_.mul(r_mean[0:1, 512:1024], ps[1][0:1, :], 1.0 / D))
        em.inc("scalar", "sca", lambda E_: E_.mul(r_ex2[0:1, 0:512], ps[2][0:1, :], 1.0 / D))
        c4 = em.inc("scalar", "sca", lambda E_: E_.mul(r_ex2[0:1, 512:1024], ps[3][0:1, :], 1.0 / D))
        em.wge("scalar", "sca", c2)
        msq = em.inc("scalar", "sca", lambda E_: E_.square(r_tmp[0:1, :], r_mean[0:1, :]))
        em.wge("vector", "sca", max(msq, c4))
        vvar = em.inc("vector", "vec", lambda E_: E_.tensor_sub(r_var[0:1, :], r_ex2[0:1, :], r_tmp[0:1, :]))
        em.wge("scalar", "vec", vvar)
        cl = em.inc("scalar", "sca", lambda E_: E_.activation(r_tmp[0:1, :], r_var[0:1, :], AF.Ln, bias=EPS))
        em.wge("scalar", "sca", cl)
        cr = em.inc("scalar", "sca", lambda E_: E_.activation(r_rstd[0:1, :], r_tmp[0:1, :], AF.Exp, scale=-0.5))
        em.wge("vector", "sca", cr)
        vmr = em.inc("vector", "vec", lambda E_: E_.tensor_mul(r_mr[0:1, :], r_mean[0:1, :], r_rstd[0:1, :]))
        em.wge("scalar", "vec", vmr)
        em.inc("scalar", "sca", lambda E_: E_.copy(r_rstd_bf[0:1, :], r_rstd[0:1, :]))
        rows_done = em.inc("scalar", "sca", lambda E_: E_.copy(r_mr_bf[0:1, :], r_mr[0:1, :]))
        em.wge("tensor", "sca", rows_done)
        bc_done = None
        for i, (bank, row) in enumerate([(0, r_rstd_bf), (1, r_rstd_bf), (2, r_mr_bf), (3, r_mr_bf)]):
            tn = i % 2
            b = lambda E_, bank=bank, row=row, tn=tn: E_.matmul(
                ps[bank][:, :], ones_row[:, :], row[0:1, tn * 512:(tn + 1) * 512],
                start=True, stop=True)
            if i == 3:
                bc_done = em.inc("tensor", "mm", b)
            else:
                em.run("tensor", b)
        em.wge("vector", "mm", bc_done)
        em.wge("vector", "sx", state["x_gpd"])
        xn_done = None
        sub_war = {}
        for dt in range(DT):
            par = dt % 2
            em.wge("vector", "vec", sub_war.get(par, 0))
            last_mul = None
            for tn in range(2):
                last_mul = em.inc("vector", "vec", lambda E_, dt=dt, tn=tn, par=par: E_.tensor_mul(
                    ln_t[par][:, tn * 512:(tn + 1) * 512],
                    X[:, dt * TOK + tn * 512: dt * TOK + (tn + 1) * 512],
                    ps[tn][:, :]))
            em.wge("vector", "vec", last_mul)
            for tn in range(2):
                xn_done = em.inc("vector", "vec", lambda E_, dt=dt, tn=tn, par=par: E_.tensor_sub(
                    XN[:, dt * TOK + tn * 512: dt * TOK + (tn + 1) * 512],
                    ln_t[par][:, tn * 512:(tn + 1) * 512], ps[2 + tn][:, :]))
            sub_war[par] = xn_done
        for i in range(4):
            bank_free[i] = [("vec", xn_done)]
        return xn_done

    def router(xn_done):
        lg_war = [0]
        em.wge("vector", "mm", ffn_state.get("w16_mm", 0) if isinstance(ffn_state, dict) else 0)
        for tt in range(8):
            bank = 4 + tt % 2
            builds = []
            for kt in range(16):
                builds.append(lambda E_, tt=tt, kt=kt, bank=bank: E_.matmul(
                    ps[bank][:, 0:E],
                    XN[:, kt * TOK + tt * 128: kt * TOK + (tt + 1) * 128],
                    rw_sb[:, kt * E:(kt + 1) * E],
                    start=(kt == 0), stop=(kt == 15)))
            done = mm_group(bank, builds,
                            extra_waits=[("vec", xn_done), ("scst", state["ph0_gpd"])])
            em.wge("vector", "mm", done)
            em.wge("vector", "sca", lg_war[0])
            vc0 = em.inc("vector", "vec", lambda E_, bank=bank: E_.tensor_copy(
                lg_sb[:, :], ps[bank][:, 0:E]))
            bank_free[4 + tt % 2] = [("vec", vc0)]
            em.wge("vector", "vec", vc0)
            vmax = em.inc("vector", "vec", lambda E_: E_.max(m8_sb[:, :], lg_sb[:, :]))
            em.wge("vector", "vec", vmax)
            em.inc("vector", "vec", lambda E_: E_.tensor_scalar_mul(nm1_sb[:, :], m8_sb[:, 0:1], -1.0))
            vc1 = em.inc("vector", "vec", lambda E_: E_.tensor_sub(e2a_sb[:, :], m8_sb[:, 1:2], m8_sb[:, 0:1]))
            em.wge("scalar", "vec", vc1)
            em.inc("scalar", "sca", lambda E_: E_.activation(
                sexp_sb[:, :], lg_sb[:, :], AF.Exp, bias=nm1_sb[:, 0:1]))
            sc1 = em.inc("scalar", "sca", lambda E_: E_.activation(
                e2v_sb[:, :], e2a_sb[:, :], AF.Exp))
            lg_war[0] = sc1
            em.wge("vector", "sca", sc1)
            va = em.inc("vector", "vec", lambda E_: E_.tensor_scalar_add(e2v_sb[:, :], e2v_sb[:, :], 1.0))
            em.wge("vector", "vec", va)
            vr = em.inc("vector", "vec", lambda E_: E_.reciprocal(fac_sb[:, :], e2v_sb[:, :]))
            em.wge("vector", "vec", vr)
            em.inc("vector", "vec", lambda E_: E_.tensor_scalar_mul(fac_sb[:, :], fac_sb[:, :], 0.5))
            vm = em.inc("vector", "vec", lambda E_: E_.tensor_scalar(
                msk_sb[:, :], lg_sb[:, :], m8_sb[:, 1:2], None, OP.is_ge))
            em.wge("vector", "vec", vm)
            em.inc("vector", "vec", lambda E_, tt=tt: E_.scalar_tensor_tensor(
                W16[:, tt * E:(tt + 1) * E], sexp_sb[:, :], fac_sb[:, 0:1], msk_sb[:, :],
                OP.mult, OP.mult))
        return em.cnt["vec"]

    wrow_free = {0: 0, 1: 0}
    wbc_war = {0: 0, 1: 0}

    def wbc_chain(u, w16_done):
        par = u % 2
        em.wge_multi("tensor", bank_free["H"])
        em.wge("tensor", "vec", w16_done)
        em.wge("tensor", "gp", state["consts_gp"])
        tr_done = None
        for tt in range(8):
            b = lambda E_, tt=tt, u=u: E_.transpose(
                psH[0:1, tt * 128:(tt + 1) * 128],
                W16[:, tt * E + u: tt * E + u + 1], ident[:, :])
            if tt == 7:
                tr_done = em.inc("tensor", "mm", b)
            else:
                em.run("tensor", b)
        em.wge("scalar", "mm", max(tr_done, wrow_free[par]))
        cp_done = em.inc("scalar", "sca", lambda E_, par=par: E_.copy(
            wrow_b[par][0:1, :], psH[0:1, 0:TOK]))
        bank_free["H"] = [("sca", cp_done)]
        last_vec = None
        for tn in range(2):
            em.wge_multi("tensor", bank_free["G"])
            em.wge("tensor", "sca", cp_done)
            done = em.inc("tensor", "mm", lambda E_, tn=tn, par=par: E_.matmul(
                psG[:, :], ones_row[:, :], wrow_b[par][0:1, tn * 512:(tn + 1) * 512],
                start=True, stop=True))
            em.wge("vector", "mm", done)
            em.wge("vector", "vec", wbc_war[par])
            last_vec = em.inc("vector", "vec", lambda E_, tn=tn, par=par: E_.tensor_copy(
                wbc_b[par][:, tn * 512:(tn + 1) * 512], psG[:, :]))
            bank_free["G"] = [("vec", last_vec)]
        wrow_free[par] = em.cnt["mm"]
        ffn_state["w16_mm"] = em.cnt["mm"]
        return last_vec

    ffn_state = {"gslot": 0, "uslot": 0, "dslot": 0, "gp_par": 0, "d_par": 0,
                 "gfree": {}, "ufree": {}, "dfree": {}, "hp_free": 0,
                 "sl_war": {0: 0, 1: 0}, "t1_war": {0: 0, 1: 0}, "x_last": 0}

    def ffn_pass(xn_done):
        st = ffn_state
        w16_done = router(xn_done)
        order = [16, 17] + list(range(16))
        wbc_ready = {}

        def stream_gu_slot(u, fb):
            for ring, d_src, key, freekey, wname in (
                    (gw_ring, d_gw, "gslot", "gfree", "wg"),
                    (uw_ring, d_uw, "uslot", "ufree", "wu")):
                s = st[key]
                if s >= 4:
                    em.wge("gpsimd", "mm", st[freekey][s - 4])
                    em.wge("gpsimd", f"{wname}{s % 4}", 16 * (s // 4))
                em.inc("gpsimd", f"{wname}{s % 4}",
                       lambda E_, u=u, fb=fb, s=s, ring=ring, d_src=d_src:
                       E_.dma_start(ring[:, (s % 4) * 2048:(s % 4 + 1) * 2048], d_src[u, fb]), 16)
                st[key] = s + 1

        def stream_d_slot(u, db):
            s = st["dslot"]
            if s >= 4:
                em.wge("gpsimd", "mm", st["dfree"][s - 4])
                em.wge("gpsimd", f"wd{s % 4}", 16 * (s // 4))
            em.inc("gpsimd", f"wd{s % 4}", lambda E_, u=u, db=db, s=s:
                   E_.dma_start(dw_ring[:, (s % 4) * 1024:(s % 4 + 1) * 1024], d_dw[u, db]), 16)
            st["dslot"] = s + 1

        for idx, u in enumerate(order):
            base_g = st["gslot"]
            base_d = st["dslot"]
            is_shared = u >= 16
            wbc = wbc_half if is_shared else wbc_b[u % 2]
            first_silu = True
            for fb in range(8):
                gs = base_g + fb
                stream_gu_slot(u, fb)
                for tn in range(2):
                    p = st["gp_par"] % 2
                    st["gp_par"] += 1
                    pg, pu = ps[p], ps[2 + p]
                    builds = []
                    for kt in range(16):
                        builds.append(lambda E_, gs=gs, kt=kt, tn=tn, pg=pg: E_.matmul(
                            pg[:, :],
                            gw_ring[:, (gs % 4) * 2048 + kt * 128:(gs % 4) * 2048 + (kt + 1) * 128],
                            XN[:, kt * TOK + tn * 512: kt * TOK + (tn + 1) * 512],
                            start=(kt == 0), stop=(kt == 15)))
                    g_done = mm_group(p, builds,
                                      extra_waits=[(f"wg{gs % 4}", 16 * (gs // 4 + 1)), ("vec", xn_done)])
                    if tn == 1:
                        st["gfree"][gs] = g_done
                    builds = []
                    for kt in range(16):
                        builds.append(lambda E_, gs=gs, kt=kt, tn=tn, pu=pu: E_.matmul(
                            pu[:, :],
                            uw_ring[:, (gs % 4) * 2048 + kt * 128:(gs % 4) * 2048 + (kt + 1) * 128],
                            XN[:, kt * TOK + tn * 512: kt * TOK + (tn + 1) * 512],
                            start=(kt == 0), stop=(kt == 15)))
                    u_done = mm_group(2 + p, builds, extra_waits=[(f"wu{gs % 4}", 16 * (gs // 4 + 1))])
                    if tn == 1:
                        st["ufree"][gs] = u_done
                    em.wge("scalar", "mm", g_done)
                    em.wge("scalar", "vec", st["sl_war"][p])
                    sl_done = em.inc("scalar", "sca", lambda E_, p=p, pg=pg: E_.activation(
                        sl_b[p][:, :], pg[:, :], AF.Silu))
                    if first_silu:
                        em.wge("vector", "mm", st["hp_free"])
                        if is_shared:
                            em.wge("vector", "gp", state["consts_gp"])
                        else:
                            em.wge("vector", "vec", wbc_ready[u])
                        first_silu = False
                    em.wge("vector", "mm", u_done)
                    em.wge("vector", "sca", sl_done)
                    em.wge("vector", "vec", st["t1_war"][p])
                    vt1 = em.inc("vector", "vec", lambda E_, p=p, pu=pu: E_.tensor_mul(
                        t1_b[p][:, :], pu[:, :], sl_b[p][:, :]))
                    st["sl_war"][p] = vt1
                    bank_free[p] = [("sca", sl_done)]
                    bank_free[2 + p] = [("vec", vt1)]
                    em.wge("vector", "vec", vt1)
                    vhp = em.inc("vector", "vec", lambda E_, p=p, fb=fb, tn=tn, wbc=wbc: E_.tensor_mul(
                        hp[:, fb * TOK + tn * 512: fb * TOK + (tn + 1) * 512],
                        t1_b[p][:, :], wbc[:, tn * 512:(tn + 1) * 512]))
                    st["t1_war"][p] = vhp
            hp_done = em.cnt["vec"]
            if not is_shared:
                wbc_war[u % 2] = hp_done
            if idx + 2 < len(order):
                nxt = order[idx + 2]
                if nxt < 16:
                    wbc_ready[nxt] = wbc_chain(nxt, w16_done)
            x_wait = st["x_last"]
            for db in range(16):
                dsq = base_d + db
                stream_d_slot(u, db)
                for tn in range(2):
                    p = st["d_par"] % 2
                    st["d_par"] += 1
                    pd = ps[4 + p]
                    builds = []
                    for kf in range(8):
                        builds.append(lambda E_, dsq=dsq, kf=kf, tn=tn, pd=pd: E_.matmul(
                            pd[:, :],
                            dw_ring[:, (dsq % 4) * 1024 + kf * 128:(dsq % 4) * 1024 + (kf + 1) * 128],
                            hp[:, kf * TOK + tn * 512: kf * TOK + (tn + 1) * 512],
                            start=(kf == 0), stop=(kf == 7)))
                    d_done = mm_group(4 + p, builds,
                                      extra_waits=[(f"wd{dsq % 4}", 16 * (dsq // 4 + 1)), ("vec", hp_done)])
                    if tn == 1:
                        st["dfree"][dsq] = d_done
                    em.wge("vector", "mm", d_done)
                    em.wge("vector", "vec", x_wait)
                    vc = em.inc("vector", "vec", lambda E_, db=db, tn=tn, pd=pd: E_.tensor_tensor(
                        X[:, db * TOK + tn * 512: db * TOK + (tn + 1) * 512],
                        X[:, db * TOK + tn * 512: db * TOK + (tn + 1) * 512],
                        pd[:, :], OP.add))
                    bank_free[4 + p] = [("vec", vc)]
            st["x_last"] = em.cnt["vec"]
            st["hp_free"] = em.cnt["mm"]
        return st["x_last"]

    def xdbl(xn_done):
        for tn in range(2):
            builds = []
            for kt in range(16):
                builds.append(lambda E_, kt=kt, tn=tn: E_.matmul(
                    ps[tn][:, :], xpw_sb[:, kt * 160: kt * 160 + 128],
                    XN[:, kt * TOK + tn * 512: kt * TOK + (tn + 1) * 512],
                    start=(kt == 0), stop=(kt == 15)))
            done = mm_group(tn, builds,
                            extra_waits=[("vec", xn_done), ("scst", state["ph0_gpd"])])
            em.wge("vector", "mm", done)
            vc = em.inc("vector", "vec", lambda E_, tn=tn: E_.tensor_copy(
                delta_rT[:, tn * 512:(tn + 1) * 512], ps[tn][:, :]))
            bank_free[tn] = [("vec", vc)]
        for tn in range(2):
            builds = []
            for kt in range(16):
                builds.append(lambda E_, kt=kt, tn=tn: E_.matmul(
                    ps[2 + tn][0:32, :], xpw_sb[:, kt * 160 + 128: kt * 160 + 160],
                    XN[:, kt * TOK + tn * 512: kt * TOK + (tn + 1) * 512],
                    start=(kt == 0), stop=(kt == 15)))
            done = mm_group(2 + tn, builds)
            em.wge("vector", "mm", done)
            vc = em.inc("vector", "vec", lambda E_, tn=tn: E_.tensor_copy(
                bc_sb[0:32, tn * 512:(tn + 1) * 512], ps[2 + tn][0:32, :]))
            bank_free[2 + tn] = [("vec", vc)]
        bc_v = em.cnt["vec"]
        em.wge("gpsimd", "vec", bc_v)
        bc_w = em.inc("gpsimd", "gpd", lambda E_: E_.dma_start(bc_dram.ap(), bc_sb[0:32, :]), 16)
        return bc_w

    def mamba_pass(which, bc_w, hmz_ready=0):
        dA_free = {}
        dbu_free = {}
        bbc_free = {}
        cbc_free = {}
        em.wge("sync", "gpd", bc_w)
        em.wge("vector", "syn", state["ph0_syn"])
        em.wge("vector", "vec", hmz_ready)
        em.wge("tensor", "scst", state["ph0_gpd"])
        em.wge("scalar", "sca", state["a_sca"])
        hf_last = 0
        prev_vd = 0
        dA_last = 0
        dbu_last = ("gp", 0)
        for dt in range(DT):
            delta_done = None
            for tn in range(2):
                pe = ps[4 + tn]
                d_done = mm_group(4 + tn, [lambda E_, dt=dt, tn=tn, pe=pe: E_.matmul(
                    pe[:, :], dtw_sb[:, dt * 128:(dt + 1) * 128],
                    delta_rT[:, tn * 512:(tn + 1) * 512],
                    start=True, stop=True)])
                em.wge("vector", "mm", d_done)
                em.wge("vector", "vec", em.cnt["vec"])
                em.wge("vector", "sca", em.cnt["sca"])
                vz = em.inc("vector", "vec", lambda E_, dt=dt, pe=pe: E_.tensor_scalar(
                    tmp_m[:, 0:512], pe[:, :], dtb_sb[:, dt:dt + 1], None, OP.add))
                em.wge("vector", "vec", vz)
                vzr = em.inc("vector", "vec", lambda E_: E_.tensor_scalar(
                    tmp_m[:, 512:1024], tmp_m[:, 0:512], 0.0, None, OP.max))
                em.wge("scalar", "vec", vz)
                em.wge("scalar", "sca", em.cnt["sca"])
                caz = em.inc("scalar", "sca", lambda E_: E_.activation(
                    l1p_m[:, :], tmp_m[:, 0:512], AF.Abs))
                em.wge("scalar", "sca", caz)
                ce = em.inc("scalar", "sca", lambda E_: E_.activation(
                    e_m[:, :], l1p_m[:, :], AF.Exp, scale=-1.0))
                em.wge("scalar", "sca", ce)
                em.wge("scalar", "vec", prev_vd)
                sc = em.inc("scalar", "sca", lambda E_: E_.activation(
                    l1p_m[:, :], e_m[:, :], AF.Ln, bias=1.0))
                em.wge("vector", "sca", sc)
                em.wge("vector", "vec", vzr)
                em.wge("vector", "sca", dA_last)
                em.wge("vector", dbu_last[0], dbu_last[1])
                vd = em.inc("vector", "vec", lambda E_, tn=tn: E_.tensor_add(
                    delta_sb[:, tn * 512:(tn + 1) * 512], tmp_m[:, 512:1024], l1p_m[:, :]))
                prev_vd = vd
                em.wge("vector", "vec", vd)
                delta_done = em.inc("vector", "vec", lambda E_, dt=dt, tn=tn: E_.tensor_mul(
                    du_sb[:, tn * 512:(tn + 1) * 512],
                    delta_sb[:, tn * 512:(tn + 1) * 512],
                    XN[:, dt * TOK + tn * 512: dt * TOK + (tn + 1) * 512]))
                bank_free[4 + tn] = [("vec", delta_done), ("sca", ce)]
            for n in range(NST):
                par = n % 2
                em.wge("sync", "gp", bbc_free.get(par, 0))
                em.wge("sync", f"bb{par}", em.cnt[f"bb{par}"])
                bsrc = bass.AP(bc_dram, n * TOK, [[0, 128], [1, TOK]])
                b_ld = em.inc("sync", f"bb{par}", lambda E_, par=par, bsrc=bsrc: E_.dma_start(
                    bbcb[par][:, :], bsrc), 16)
                c_ld = None
                if which == "B":
                    em.wge("sync", "vec", cbc_free.get(par, 0))
                    em.wge("sync", f"cb{par}", em.cnt[f"cb{par}"])
                    csrc = bass.AP(bc_dram, (16 + n) * TOK, [[0, 128], [1, TOK]])
                    c_ld = em.inc("sync", f"cb{par}", lambda E_, par=par, csrc=csrc: E_.dma_start(
                        cbcb[par][:, :], csrc), 16)
                em.wge("scalar", "vec", max(delta_done, dA_free.get(par, 0)))
                sc = em.inc("scalar", "sca", lambda E_, dt=dt, n=n, par=par: E_.activation(
                    dAb[par][:, :], delta_sb[:, :], AF.Exp,
                    scale=A_sb[:, dt * NST + n: dt * NST + n + 1]))
                dA_last = sc
                on_vec = (which == "A" and n % 2 == 1)
                if on_vec:
                    em.wge("vector", f"bb{par}", b_ld)
                    em.wge("vector", "vec", max(delta_done, dbu_free.get(par, 0)))
                    gc = em.inc("vector", "vec", lambda E_, par=par: E_.tensor_mul(
                        dbub[par][:, :], du_sb[:, :], bbcb[par][:, :]))
                    em.wge("gpsimd", "vec", gc)
                    gv = em.inc("gpsimd", "gp", lambda E_: E_.engine_nop())
                    bbc_free[par] = gv
                    dbu_last = ("vec", gc)
                else:
                    em.wge("gpsimd", f"bb{par}", b_ld)
                    em.wge("gpsimd", "vec", max(delta_done, dbu_free.get(par, 0)))
                    gc = em.inc("gpsimd", "gp", lambda E_, par=par: E_.tensor_mul(
                        dbub[par][:, :], du_sb[:, :], bbcb[par][:, :]))
                    bbc_free[par] = gc
                    dbu_last = ("gp", gc)
                em.wge("vector", "sca", sc)
                if not on_vec:
                    em.wge("vector", "gp", gc)
                em.wge("vector", "vec", max(dA_free.get(0, 0), dA_free.get(1, 0)))
                if which == "A":
                    vs = em.inc("vector", "vec", lambda E_, par=par: E_.tensor_tensor_scan(
                        h_sb[:, :], dAb[par][:, :], dbub[par][:, :], 0.0, OP.mult, OP.add))
                    em.wge("vector", "vec", vs)
                    vc = em.inc("vector", "vec", lambda E_, dt=dt, n=n: E_.tensor_copy(
                        hfinA[:, dt * NST + n: dt * NST + n + 1], h_sb[:, TOK - 1:TOK]))
                    hf_last = vc
                else:
                    vs = em.inc("vector", "vec", lambda E_, par=par, dt=dt, n=n: E_.tensor_tensor_scan(
                        h_sb[:, :], dAb[par][:, :], dbub[par][:, :],
                        hmz_sb[:, dt * NST + n: dt * NST + n + 1], OP.mult, OP.add))
                    em.wge("vector", f"cb{par}", c_ld)
                    em.wge("vector", "vec", vs)
                    if n == 0:
                        vc = em.inc("vector", "vec", lambda E_, par=par: E_.tensor_mul(
                            y_sb[:, :], h_sb[:, :], cbcb[par][:, :]))
                    else:
                        vt = em.inc("vector", "vec", lambda E_, par=par: E_.tensor_mul(
                            tmp_bf[:, :], h_sb[:, :], cbcb[par][:, :]))
                        em.wge("vector", "vec", vt)
                        vc = em.inc("vector", "vec", lambda E_: E_.tensor_add(
                            y_sb[:, :], y_sb[:, :], tmp_bf[:, :]))
                    cbc_free[par] = vc
                dA_free[par] = vc
                dbu_free[par] = vc
            if which == "B":
                em.wge("vector", "vec", em.cnt["vec"])
                vst = em.inc("vector", "vec", lambda E_, dt=dt: E_.scalar_tensor_tensor(
                    tmp_m[:, :], XN[:, dt * TOK:(dt + 1) * TOK],
                    dvec_sb[:, dt:dt + 1], y_sb[:, :], OP.mult, OP.add))
                em.wge("vector", "vec", vst)
                em.inc("vector", "vec", lambda E_, dt=dt: E_.tensor_tensor(
                    X[:, dt * TOK:(dt + 1) * TOK], X[:, dt * TOK:(dt + 1) * TOK],
                    tmp_m[:, :], OP.add))
        return em.cnt["vec"] if which == "B" else hf_last

    def exchange(hf_last):
        em.wge("gpsimd", "vec", hf_last)
        c0 = em.cnt["gpd"]

        gpd_sem = em.sem["gpd"]

        def _send(g):
            for arm in g.switch_core_id(n=128):
                c = int(arm.logical)
                g.dma_start(sh_h.ap()[c % 2], hfinA[:, :]).then_inc(gpd_sem, 16)
        em.cnt["gpd"] += 16
        em.run("gpsimd", _send)
        em.wge("gpsimd", "gpd", c0 + 16)

        def _set_flag(g):
            for arm in g.switch_core_id(n=128):
                c = int(arm.logical)
                g.dma_start(sh_f.ap()[c % 2], mg_sb[:, :]).then_inc(gpd_sem, 16)
        em.cnt["gpd"] += 16
        em.run("gpsimd", _set_flag)
        em.wge("gpsimd", "gpd", c0 + 32)

        poll_sem = em.sem["poll"]

        def _poll(g):
            with g.register("rf") as rf, g.register("rt") as rt, g.register("rv") as rv:
                g.reg_mov(rf, 1)
                g.reg_mov(rt, 0)
                with g.While(rf):
                    g.reg_add(rt, rt, 16)
                    for arm in g.switch_core_id(n=128):
                        c = int(arm.logical)
                        g.dma_start(pf_sb[:, :], sh_f.ap()[1 - c % 2]).then_inc(poll_sem, 16)
                    g.wait_ge(poll_sem, rt)
                    g.reg_load(rv, pf_sb[0:1, 0:1])
                    g.reg_alu(rf, rv, MAGIC, OP.bitwise_xor)
        em.run("gpsimd", _poll)

        def _recv(g):
            for arm in g.switch_core_id(n=128):
                c = int(arm.logical)
                g.dma_start(hm_sb[:, :], sh_h.ap()[1 - c % 2]).then_inc(gpd_sem, 16)
        em.cnt["gpd"] += 16
        em.run("gpsimd", _recv)
        em.wge("gpsimd", "gpd", c0 + 48)
        gp_done = em.inc("gpsimd", "gp", lambda E_: E_.engine_nop())
        em.wge("vector", "gp", gp_done)
        em.wge("vector", "syn", state["ph0_syn"])
        return em.inc("vector", "vec", lambda E_: E_.tensor_scalar_mul(
            hmz_sb[:, :], hm_sb[:, :], flagO_sb[:, 0:1]))

    def store(x_ready):
        import os
        tap = os.environ.get("MK_TAP", "X")
        src_t = {"X": X, "XN": XN, "W16": W16, "HP": hp, "HFA": hfinA,
                 "DRT": delta_rT, "BC": bc_sb, "DBG": lg_sb, "DELTA": delta_sb,
                 "DU": du_sb, "H": h_sb, "DA0": dA0, "DA1": dA1,
                 "DBU0": dbu0, "DBU1": dbu1, "Y": y_sb, "HMZ": hmz_sb}.get(tap, X)
        em.wge("gpsimd", "vec", x_ready)
        done = em.cnt["gpd"]
        if tap in ("X", "XN", "HP"):
            nk = 4 if tap != "HP" else 2
            for k in range(nk):
                done = em.inc("gpsimd", "gpd", lambda E_, k=k, src_t=src_t: E_.dma_start(
                    d_out[:, k * 4096:(k + 1) * 4096], src_t[:, k * 4096:(k + 1) * 4096]), 16)
        elif tap == "BC":
            done = em.inc("gpsimd", "gpd", lambda E_: E_.dma_start(
                d_out[0:32, 0:TOK], bc_sb[:, :]), 16)
        else:
            w = src_t.shape[1]
            done = em.inc("gpsimd", "gpd", lambda E_, src_t=src_t, w=w: E_.dma_start(
                d_out[:, 0:w], src_t[:, :]), 16)
        em.wge("gpsimd", "gpd", done)

    x_ready = 0
    ph0()
    em.flush("ph0")
    if "ln1" in phases:
        layernorm(x_ready)
        em.flush("ln1")
    if "rt" in phases:
        xn = layernorm(x_ready)
        router(xn)
        em.flush("rt")
    if "ffn1" in phases:
        xn = layernorm(x_ready)
        x_ready = ffn_pass(xn)
        em.flush("ffn1")
    if "mamba" in phases:
        xn = layernorm(x_ready)
        bc_w = xdbl(xn)
        em.flush("xdbl")
        hf = mamba_pass("A", bc_w)
        em.flush("mambaA")
        hmz_ready = 0
        if "cc" in phases:
            hmz_ready = exchange(hf)
            em.flush("cc")
        x_ready = mamba_pass("B", bc_w, hmz_ready)
        em.flush("mambaB")
    if "ffn2" in phases:
        xn = layernorm(x_ready)
        x_ready = ffn_pass(xn)
        em.flush("ffn2")
    store(x_ready)
    em.flush("store")
    return nc


def _prep_inputs(L0_attn_x, L0_attn_A_log, L0_attn_D, L0_attn_x_proj_w,
                 L0_attn_dt_proj_w, L0_attn_dt_proj_b, L0_ffn_router_w,
                 L0_ffn_gate_w, L0_ffn_up_w, L0_ffn_down_w,
                 L0_ffn_shared_gate_w, L0_ffn_shared_up_w, L0_ffn_shared_down_w):
    f = np.float32
    x = np.asarray(L0_attn_x, dtype=f)
    gate_w = np.asarray(L0_ffn_gate_w, dtype=f)
    up_w = np.asarray(L0_ffn_up_w, dtype=f)
    down_w = np.asarray(L0_ffn_down_w, dtype=f)
    sgw = np.asarray(L0_ffn_shared_gate_w, dtype=f)
    suw = np.asarray(L0_ffn_shared_up_w, dtype=f)
    sdw = np.asarray(L0_ffn_shared_down_w, dtype=f)

    def gu_pack(w):
        return w.reshape(16, 16, 128, 8, 128).transpose(0, 3, 2, 1, 4).reshape(16, 8, 128, 2048)

    def gu_pack_shared(w):
        return w.reshape(16, 128, 2, 8, 128).transpose(2, 3, 1, 0, 4).reshape(2, 8, 128, 2048)

    gw = np.ascontiguousarray(np.concatenate([gu_pack(gate_w), gu_pack_shared(sgw)], axis=0))
    uw = np.ascontiguousarray(np.concatenate([gu_pack(up_w), gu_pack_shared(suw)], axis=0))

    def d_pack(w):
        return w.reshape(16, 8, 128, 16, 128).transpose(0, 3, 2, 1, 4).reshape(16, 16, 128, 1024)

    def d_pack_shared(w):
        return w.reshape(2, 8, 128, 16, 128).transpose(0, 3, 2, 1, 4).reshape(2, 16, 128, 1024)

    dw = np.ascontiguousarray(np.concatenate([d_pack(down_w), d_pack_shared(sdw)], axis=0))

    rw = np.ascontiguousarray(np.asarray(L0_ffn_router_w, dtype=f).T.reshape(16, 128, 16))
    xpw = np.ascontiguousarray(np.asarray(L0_attn_x_proj_w, dtype=f).T.reshape(16, 128, 160))
    dtw = np.ascontiguousarray(np.asarray(L0_attn_dt_proj_w, dtype=f).T)
    alog = np.ascontiguousarray(
        np.asarray(L0_attn_A_log, dtype=f).reshape(16, 128, 16).transpose(1, 0, 2).reshape(128, 256))
    dvec = np.ascontiguousarray(np.asarray(L0_attn_D, dtype=f).reshape(16, 128).T)
    dtb = np.ascontiguousarray(np.asarray(L0_attn_dt_proj_b, dtype=f).reshape(16, 128).T)
    consts = np.tile(np.asarray(CONST_VALS, dtype=f), (128, 1))
    magic = np.full((1, 16), MAGIC, np.int32)

    shared = dict(gw=gw, uw=uw, dw=dw, rw=rw, xpw=xpw, dtw=dtw,
                  alog=alog, dvec=dvec, dtb=dtb, consts=consts, magic=magic)
    in_maps = []
    for c in range(NCORES):
        b, h = c // 2, c % 2
        xc = np.ascontiguousarray(
            x[b, h * 1024:(h + 1) * 1024, :].T.reshape(16, 128, 1024)
            .transpose(1, 0, 2).reshape(128, 16 * 1024))
        m = dict(shared)
        m.update(xt=xc,
                 flagE=np.full((128, 1), 1.0 - h, dtype=f),
                 flagO=np.full((128, 1), float(h), dtype=f))
        in_maps.append(m)
    return in_maps


_NC_CACHE = {}


def kernel(**inputs):
    import os
    phases = tuple(os.environ.get("MK_PHASES", "ffn1,mamba,cc,ffn2").split(","))
    phases = phases + (os.environ.get("MK_TAP", "X"),)
    in_maps = _prep_inputs(**inputs)
    if _NC_CACHE.get("phases") != phases:
        _NC_CACHE["nc"] = build_program(phases)
        _NC_CACHE["phases"] = phases
    nc = _NC_CACHE["nc"]
    res = run_bass_kernel_spmd(nc, in_maps, core_ids=list(range(NCORES)))
    out = np.empty((4, 2048, 2048), dtype=np.float32)
    for c in range(NCORES):
        b, h = c // 2, c % 2
        oc = np.asarray(res.results[c]["out"])
        out[b, h * 1024:(h + 1) * 1024, :] = (
            oc.reshape(128, 16, 1024).transpose(1, 0, 2).reshape(2048, 1024).T)
    return out


# version 04 (final)
# speedup vs baseline: 3.6346x; 3.6346x over previous
import numpy as np
import concourse.bass as bass
import concourse.mybir as mybir
from bass_rust import IndirectOffsetOnAxis as IOA
from concourse.bass_utils import run_bass_kernel_spmd

F32 = mybir.dt.float32
I32 = mybir.dt.int32
I16 = mybir.dt.int16
BF16 = mybir.dt.bfloat16
AF = mybir.ActivationFunctionType
OP = mybir.AluOpType

NCORES = 8
D = 2048
TOK = 1024
DT = 16
NST = 16
E = 16
NU = 18
CAP = 256
EPS = 1e-5
MAGIC = 0x5A5A1234
CONST_VALS = (EPS, 1.0, -0.5, -1.0, 1.0 / 2048.0, 0.5, 0.0, 2.0)

ENGINES = ("tensor", "vector", "scalar", "gpsimd", "sync")
SEM_OWNER = {"mm": "tensor", "vec": "vector", "sca": "scalar", "gp": "gpsimd",
             "syn": None, "gpd": None, "sx": None, "scst": None,
             "wg0": None, "wg1": None, "wg2": None, "wg3": None,
             "wu0": None, "wu1": None, "wu2": None, "wu3": None,
             "wd0": None, "wd1": None, "wd2": None, "wd3": None,
             "bb0": None, "bb1": None, "cb0": None, "cb1": None,
             "poll": None, "xw": None, "xw1": None, "ix": None, "ga": None, "wx": None,
             "tr": None, "sc": None, "rz": None, "rm": None, "rm1": None}


class Em:

    def __init__(self, nc):
        self.nc = nc
        self.q = {e: [] for e in ENGINES}
        self.sem = {}
        self.cnt = {}
        self.last_wait = {e: {} for e in ENGINES}
        for name in SEM_OWNER:
            self.sem[name] = nc.alloc_semaphore("s_" + name)
            self.cnt[name] = 0

    def emit(self, eng, fn):
        self.q[eng].append(fn)

    def wge(self, eng, name, tgt):
        if tgt <= 0 or (eng == "tensor" and name == "mm"):
            return
        lw = self.last_wait[eng]
        if lw.get(name, 0) >= tgt:
            return
        lw[name] = tgt
        sem = self.sem[name]
        self.emit(eng, lambda E_: E_.wait_ge(sem, tgt))

    def wge_multi(self, eng, waits):
        for name, tgt in waits:
            self.wge(eng, name, tgt)

    def inc(self, eng, name, build, n=1):
        self.cnt[name] += n
        sem = self.sem[name]
        w = lambda E_: build(E_).then_inc(sem, n)
        w._inner = build
        self.emit(eng, w)
        return self.cnt[name]

    def run(self, eng, build):
        self.emit(eng, lambda E_: build(E_))

    def flush(self, name=None):
        with self.nc.Block(name) as b:
            for eng in ENGINES:
                if self.q[eng]:
                    fns = self.q[eng]
                    self.q[eng] = []
                    def _runner(E_, fns=fns):
                        for f in fns:
                            try:
                                f(E_)
                            except Exception:
                                g = getattr(f, "_inner", f)
                                print("FAILED lambda line", g.__code__.co_firstlineno,
                                      "defaults", g.__defaults__)
                                raise
                    getattr(b, eng)(_runner)


def build_program(phases=("ffn1", "mamba", "cc", "ffn2", "X")):
    nc = bass.Bass("TRN2", target_bir_lowering=False)

    d_x = nc.dram_tensor("xt", [128, DT * TOK], F32, kind="ExternalInput").ap()
    d_gw = nc.dram_tensor("gw", [NU, 8, 128, 16 * 128], F32, kind="ExternalInput").ap()
    d_uw = nc.dram_tensor("uw", [NU, 8, 128, 16 * 128], F32, kind="ExternalInput").ap()
    d_dw = nc.dram_tensor("dw", [NU, 16, 128, 8 * 128], F32, kind="ExternalInput").ap()
    d_erow = nc.dram_tensor("erow", [128, 16], F32, kind="ExternalInput").ap()
    d_tok1 = nc.dram_tensor("tok1", [128, 8], F32, kind="ExternalInput").ap()
    d_msk16 = nc.dram_tensor("msk16", [16, 256], F32, kind="ExternalInput").ap()
    d_rw = nc.dram_tensor("rw", [16, 128, E], F32, kind="ExternalInput").ap()
    d_xpw = nc.dram_tensor("xpw", [16, 128, 160], F32, kind="ExternalInput").ap()
    d_dtw = nc.dram_tensor("dtw", [128, D], F32, kind="ExternalInput").ap()
    d_alog = nc.dram_tensor("alog", [128, DT * NST], F32, kind="ExternalInput").ap()
    d_dvec = nc.dram_tensor("dvec", [128, DT], F32, kind="ExternalInput").ap()
    d_dtb = nc.dram_tensor("dtb", [128, DT], F32, kind="ExternalInput").ap()
    d_flagE = nc.dram_tensor("flagE", [128, 1], F32, kind="ExternalInput").ap()
    d_flagO = nc.dram_tensor("flagO", [128, 1], F32, kind="ExternalInput").ap()
    d_consts = nc.dram_tensor("consts", [128, 8], F32, kind="ExternalInput").ap()
    d_magic = nc.dram_tensor("magic", [1, 16], I32, kind="ExternalInput").ap()
    d_out = nc.dram_tensor("out", [128, DT * TOK], F32, kind="ExternalOutput").ap()

    bc_dram = nc.dram_tensor("bc_scratch", [32, TOK], BF16)
    xnt_hbm = nc.dram_tensor("xnt_hbm", [TOK, D], BF16)
    routed_tm = nc.dram_tensor("routed_tm", [TOK, D], BF16)
    tab_hbm = nc.dram_tensor("tab_hbm", [16 * CAP, 2], F32)
    sh_h = nc.dram_tensor("sh_h", [2, 128, DT * NST], F32, addr_space="Shared")
    sh_f = nc.dram_tensor("sh_f", [2, 1, 16], I32, addr_space="Shared")

    AB = 16512
    nc.alloc_sbuf_tensor("alias_backing", [128, 14336], F32)
    at = nc.alloc_sbuf_tensor_at
    gw_ring = at("gw_ring", [128, 4 * 2048], BF16, offset=AB + 0)
    uw_ring = at("uw_ring", [128, 4 * 2048], BF16, offset=AB + 16384)
    dw_ring = at("dw_ring", [128, 4 * 1024], BF16, offset=AB + 32768)
    hp = at("hp", [128, 8 * TOK], BF16, offset=AB + 40960)
    dA0 = at("dA0", [128, TOK], F32, offset=AB + 0)
    dA1 = at("dA1", [128, TOK], F32, offset=AB + 4096)
    bbc0 = at("bbc0", [128, TOK], BF16, offset=AB + 8192)
    bbc1 = at("bbc1", [128, TOK], BF16, offset=AB + 10240)
    cbc0 = at("cbc0", [128, TOK], BF16, offset=AB + 12288)
    cbc1 = at("cbc1", [128, TOK], BF16, offset=AB + 14336)
    delta_sb = at("delta_sb", [128, TOK], F32, offset=AB + 16384)
    du_sb = at("du_sb", [128, TOK], BF16, offset=AB + 20480)
    h_sb = at("h_sb", [128, TOK], BF16, offset=AB + 22528)
    y_sb = at("y_sb", [128, TOK], BF16, offset=AB + 24576)
    dbu0 = at("dbu0", [128, TOK], BF16, offset=AB + 26624)
    dbu1 = at("dbu1", [128, TOK], BF16, offset=AB + 28672)
    tmp_m = at("tmp_m", [128, TOK], F32, offset=AB + 30720)
    tmp_bf = at("tmp_bf", [128, TOK], BF16, offset=AB + 34816)
    e_m = at("e_m", [128, 512], F32, offset=AB + 36864)
    l1p_m = at("l1p_m", [128, 512], F32, offset=AB + 38912)
    dAb = (dA0, dA1)
    bbcb = (bbc0, bbc1)
    cbcb = (cbc0, cbc1)
    dbub = (dbu0, dbu1)

    al = nc.alloc_sbuf_tensor
    X = al("X", [128, DT * TOK], BF16)
    XN = al("XN", [128, DT * TOK], BF16)
    xpw_sb = al("xpw_sb", [128, 16 * 160], BF16)
    dtw_sb = al("dtw_sb", [128, D], BF16)
    rw_sb = al("rw_sb", [128, 16 * E], BF16)
    alog_sb = al("alog_sb", [128, DT * NST], F32)
    A_sb = al("A_sb", [128, DT * NST], F32)
    dvec_sb = al("dvec_sb", [128, DT], F32)
    ndtb_sb = al("ndtb_sb", [128, DT], F32)
    dtb_sb = al("dtb_sb", [128, DT], F32)
    consts_sb = al("consts_sb", [128, 8], F32)
    for _i, _v in enumerate(CONST_VALS):
        nc.const_aps.register(F32, _v, consts_sb[:, _i:_i + 1])
    flagE_sb = al("flagE_sb", [128, 1], F32)
    flagO_sb = al("flagO_sb", [128, 1], F32)
    ident = al("ident", [128, 128], BF16)
    identF = al("identF", [16, 16], F32)
    ones_col = al("ones_col", [128, 1], BF16)
    ones_row = al("ones_row", [33, 128], BF16)
    stagT = al("stagT", [128, 2 * 2048], BF16)
    class _SqView:
        def __init__(self, par):
            self.par = par
        def __getitem__(self, key):
            p, c = key
            return stagT[p, self.par * 2048 + (c.start or 0): self.par * 2048 + (c.stop if c.stop is not None else TOK)]
    sq_view = (_SqView(0), _SqView(1))
    xg_d_b = (al("xg_d0", [128, 16 * 256], BF16),
              al("xg_d1", [128, 16 * 256], BF16))
    hpe = al("hpe", [128, 8 * 256], BF16)
    dg_sb = al("dg_sb", [128, 2 * 2048], BF16)
    wgbc_b = (al("wgbc0", [128, 256], BF16), al("wgbc1", [128, 256], BF16))
    tabsb = al("tabsb", [128, 512], F32)
    rhs16 = al("rhs16", [16, 256], BF16)
    msk16_sb = al("msk16_sb", [16, 256], BF16)
    idxf = al("idxf", [128, 256], F32)
    idxs16 = al("idxs16", [128, 256], I16)
    pairs_sb = al("pairs_sb", [128, 32], F32)
    tok1_sb = al("tok1_sb", [128, 8], F32)
    occrow = al("occrow", [1, 16], F32)
    occri = al("occri", [1, 16], I32)
    mask1_sb = al("mask1_sb", [128, 8 * 16], BF16)
    mask2_sb = al("mask2_sb", [128, 8 * 16], BF16)
    ones16 = al("ones16", [16, TOK], BF16)
    prefT = al("prefT", [16, TOK], F32)
    kte = al("kte", [128, 8 * 16], F32)
    sr_f = (al("sr_f1", [128, 8], F32), al("sr_f2", [128, 8], F32))
    sr_i = (al("sr_i1", [128, 8], I32), al("sr_i2", [128, 8], I32))
    erow_sb = al("erow_sb", [128, 16], F32)
    ia_i = al("ia_i", [128, 8], I32)
    ib_i = al("ib_i", [128, 8], I32)
    tmps_a = al("tmps_a", [128, 16], F32)
    tmps_b = al("tmps_b", [128, 8], F32)
    tmps_b16 = al("tmps_b16", [128, 16], F32)
    al_dyn = {n: al(n, [128, 8], F32) for n in ("wv0", "wv1", "ev0", "ev1", "kv0", "kv1")}
    rtd_b = (al("rtd0", [128, TOK], BF16), al("rtd1", [128, TOK], BF16))
    sl_b = (al("sl0", [128, 512], BF16), al("sl1", [128, 512], BF16))
    t1_b = (al("t10", [128, 512], BF16), al("t11", [128, 512], BF16))
    W16 = al("W16", [128, 8 * E], BF16)
    m8_sb = al("m8", [128, 8], F32)
    nm1_sb = al("nm1", [128, 1], F32)
    e2a_sb = al("e2a", [128, 1], F32)
    e2v_sb = al("e2v", [128, 1], F32)
    fac_sb = al("fac", [128, 1], F32)
    msk_sb = al("msk", [128, E], F32)
    sexp_sb = al("sexp", [128, E], F32)
    lg_sb = al("lg_sb", [128, E], F32)
    ln_t = (al("lnt0", [128, TOK], BF16), al("lnt1", [128, TOK], BF16))
    r_pack = al("r_pack", [97, TOK], F32)
    r_packb = al("r_packb", [33, TOK], BF16)
    class _RowView:
        def __init__(self, base, row):
            self.base, self.row = base, row
        def __getitem__(self, key):
            p, c = key
            stop = c.stop if c.stop is not None else TOK
            return self.base[self.row:self.row + 1, (c.start or 0):stop]
    r_mean = _RowView(r_pack, 0)
    r_ex2 = _RowView(r_pack, 32)
    r_tmp = _RowView(r_pack, 64)
    r_var = _RowView(r_pack, 32)
    r_rstd = _RowView(r_pack, 96)
    r_mr = _RowView(r_pack, 64)
    r_rstd_bf = _RowView(r_packb, 0)
    r_mr_bf = _RowView(r_packb, 32)
    mg_sb = al("mg_sb", [1, 16], I32)
    zf_sb = al("zf_sb", [1, 16], I32)
    pf_sb = al("pf_sb", [1, 16], I32)
    delta_rT = al("delta_rT", [128, TOK], BF16)
    bc_sb = al("bc_sb", [32, TOK], BF16)
    hfinA = alog_sb
    hm_sb = al("hm_sb", [128, DT * NST], BF16)
    hmz_sb = al("hmz_sb", [128, DT * NST], BF16)

    assert nc.sbuf_bytes_remaining >= 0, "SBUF over budget"

    ps = [nc.alloc_psum_tensor(f"ps{i}", [128, 512], F32) for i in range(6)]
    psG = nc.alloc_psum_tensor("psG", [128, 512], F32)
    psH = nc.alloc_psum_tensor("psH", [128, TOK], BF16)

    em = Em(nc)
    REGS = {}

    def _reg(E_, name):
        if name not in REGS:
            REGS[name] = E_.register(name).__enter__()
        return REGS[name]
    bank_free = {i: [] for i in range(6)}
    bank_free["G"] = []
    bank_free["H"] = []
    state = {"consts_gp": 0, "x_gpd": 0, "ph0_syn": 0, "ph0_gpd": 0, "a_sca": 0}

    def mm_group(bank, builds, extra_waits=()):
        em.wge_multi("tensor", bank_free[bank])
        em.wge_multi("tensor", extra_waits)
        done = None
        for i, b in enumerate(builds):
            if i == len(builds) - 1:
                done = em.inc("tensor", "mm", b)
            else:
                em.run("tensor", b)
        return done

    def ph0():
        from concourse.library_config import mlp as _mlp
        em.run("gpsimd", lambda E_: E_.load_library(_mlp))
        for k in range(4):
            em.inc("gpsimd", "sx", lambda E_, k=k: E_.dma_start(
                X[:, k * 4096:(k + 1) * 4096], d_x[:, k * 4096:(k + 1) * 4096]), 16)
        state["x_gpd"] = em.cnt["sx"]
        em.inc("gpsimd", "scst", lambda E_: E_.dma_start(
            xpw_sb[:, :].rearrange("p (a c) -> p a c", a=16),
            d_xpw.rearrange("a b c -> b a c")), 16)
        em.inc("gpsimd", "scst", lambda E_: E_.dma_start(dtw_sb[:, :], d_dtw), 16)
        em.inc("gpsimd", "scst", lambda E_: E_.dma_start(
            rw_sb[:, :].rearrange("p (a c) -> p a c", a=16),
            d_rw.rearrange("a b c -> b a c")), 16)
        em.inc("gpsimd", "scst", lambda E_: E_.dma_start(mg_sb[:, :], d_magic), 16)
        em.inc("gpsimd", "scst", lambda E_: E_.dma_start(msk16_sb[:, :], d_msk16), 16)
        state["ph0_gpd"] = em.cnt["scst"]
        em.inc("sync", "syn", lambda E_: E_.dma_start(alog_sb[:, :], d_alog), 16)
        em.inc("sync", "syn", lambda E_: E_.dma_start(dvec_sb[:, :], d_dvec), 16)
        em.inc("sync", "syn", lambda E_: E_.dma_start(dtb_sb[:, :], d_dtb), 16)
        em.inc("sync", "syn", lambda E_: E_.dma_start(flagE_sb[:, :], d_flagE), 16)
        em.inc("sync", "syn", lambda E_: E_.dma_start(consts_sb[:, :], d_consts), 16)
        syn_c = em.inc("sync", "syn", lambda E_: E_.dma_start(flagO_sb[:, :], d_flagO), 16)
        em.inc("sync", "syn", lambda E_: E_.dma_start(erow_sb[:, :], d_erow), 16)
        em.inc("sync", "syn", lambda E_: E_.dma_start(tok1_sb[:, :], d_tok1), 16)
        syn_c = em.cnt["syn"]
        state["ph0_syn"] = syn_c
        em.run("gpsimd", lambda E_: E_.memset(ones16[:, :], 1.0))
        zc = em.inc("gpsimd", "gp", lambda E_: E_.memset(zf_sb[:, :], 0))
        em.wge("gpsimd", "gp", zc)

        _gpd_sem = em.sem["gpd"]

        def _zero_flag(g):
            for arm in g.switch_core_id(n=128):
                c = int(arm.logical)
                g.dma_start(sh_f.ap()[c % 2], zf_sb[:, :]).then_inc(_gpd_sem, 16)
        em.cnt["gpd"] += 16
        em.run("gpsimd", _zero_flag)
        c = em.inc("gpsimd", "gp", lambda E_: E_.memset(ident[:, :], 0.0))
        em.wge("gpsimd", "gp", c)
        em.inc("gpsimd", "gp", lambda E_: E_.affine_select(
            out=ident[:, :], in_=ident[:, :], compare_op=OP.not_equal,
            fill=1.0, base=0, pattern=[[-1, 128]], channel_multiplier=1))
        em.run("gpsimd", lambda E_: E_.memset(ones_col[:, :], 1.0))
        em.run("gpsimd", lambda E_: E_.memset(ones_row[:, :], 1.0))
        em.run("gpsimd", lambda E_: E_.memset(xg_d_b[0][:, :], 0.0))
        em.inc("gpsimd", "gp", lambda E_: E_.memset(xg_d_b[1][:, :], 0.0))
        state["consts_gp"] = em.cnt["gp"]
        em.wge("scalar", "syn", syn_c)
        c = em.inc("scalar", "sca", lambda E_: E_.activation(A_sb[:, :], alog_sb[:, :], AF.Exp))
        em.wge("scalar", "sca", c)
        em.inc("scalar", "sca", lambda E_: E_.mul(A_sb[:, :], A_sb[:, :], -1.0))
        state["a_sca"] = em.inc("scalar", "sca", lambda E_: E_.mul(ndtb_sb[:, :], dtb_sb[:, :], -1.0))

    def layernorm(x_ready_vec):
        em.wge_multi("tensor", bank_free[0] + bank_free[1] + bank_free[2] + bank_free[3])
        em.wge_multi("tensor", [("sx", state["x_gpd"]), ("gp", state["consts_gp"]),
                                ("vec", x_ready_vec)])
        sq_war = {}
        s2_done = None
        for dt in range(DT):
            par = dt % 2
            em.wge("scalar", "mm", sq_war.get(par, 0))
            em.wge("scalar", "sx", state["x_gpd"])
            em.wge("scalar", "vec", x_ready_vec)
            sq_c = em.inc("scalar", "sca", lambda E_, dt=dt, par=par: E_.activation(
                sq_view[par][:, 0:TOK], X[:, dt * TOK:(dt + 1) * TOK], AF.Square))
            for tn in range(2):
                em.run("tensor", lambda E_, dt=dt, tn=tn: E_.matmul(
                    ps[tn][0:1, :], ones_col[:, :],
                    X[:, dt * TOK + tn * 512: dt * TOK + (tn + 1) * 512],
                    start=(dt == 0), stop=(dt == DT - 1)))
            em.wge("tensor", "sca", sq_c)
            for tn in range(2):
                b = lambda E_, dt=dt, tn=tn, par=par: E_.matmul(
                    ps[2 + tn][0:1, :], ones_col[:, :],
                    sq_view[par][:, tn * 512:(tn + 1) * 512],
                    start=(dt == 0), stop=(dt == DT - 1))
                if tn == 1:
                    s2_done = em.inc("tensor", "mm", b)
                    sq_war[par] = s2_done
                else:
                    em.run("tensor", b)
        em.wge("scalar", "mm", s2_done)
        msq = None
        for tn in range(2):
            msq = em.inc("scalar", "sca", lambda E_, tn=tn: E_.activation(
                r_tmp[0:1, tn * 512:(tn + 1) * 512], ps[tn][0:1, :],
                AF.Square, scale=1.0 / D))
        em.wge("vector", "sca", msq)
        vvar = None
        for tn in range(2):
            vvar = em.inc("vector", "vec", lambda E_, tn=tn: E_.scalar_tensor_tensor(
                r_var[0:1, tn * 512:(tn + 1) * 512], ps[2 + tn][0:1, :], 1.0 / D,
                r_tmp[0:1, tn * 512:(tn + 1) * 512], OP.mult, OP.subtract))
        em.wge("scalar", "vec", vvar)
        cl = em.inc("scalar", "sca", lambda E_: E_.activation(r_tmp[0:1, :], r_var[0:1, :], AF.Ln, bias=EPS))
        em.wge("scalar", "sca", cl)
        cr = em.inc("scalar", "sca", lambda E_: E_.activation(r_rstd[0:1, :], r_tmp[0:1, :], AF.Exp, scale=-0.5))
        em.wge("vector", "sca", cr)
        vmr = None
        for tn in range(2):
            vmr = em.inc("vector", "vec", lambda E_, tn=tn: E_.scalar_tensor_tensor(
                r_mr[0:1, tn * 512:(tn + 1) * 512], ps[tn][0:1, :], 1.0 / D,
                r_rstd[0:1, tn * 512:(tn + 1) * 512], OP.mult, OP.mult))
        em.wge("scalar", "vec", vmr)
        em.inc("scalar", "sca", lambda E_: E_.copy(r_rstd_bf[0:1, :], r_rstd[0:1, :]))
        rows_done = em.inc("scalar", "sca", lambda E_: E_.copy(r_mr_bf[0:1, :], r_mr[0:1, :]))
        em.wge("tensor", "sca", rows_done)
        bc_done = None
        for i, (bank, row) in enumerate([(0, r_rstd_bf), (1, r_rstd_bf), (2, r_mr_bf), (3, r_mr_bf)]):
            tn = i % 2
            orow = row.row
            b = lambda E_, bank=bank, row=row, tn=tn, orow=orow: E_.matmul(
                ps[bank][:, :], ones_row[orow:orow + 1, :], row[0:1, tn * 512:(tn + 1) * 512],
                start=True, stop=True)
            if i == 3:
                bc_done = em.inc("tensor", "mm", b)
            else:
                em.run("tensor", b)
        em.wge("vector", "mm", bc_done)
        em.wge("vector", "sx", state["x_gpd"])
        xn_done = None
        sub_war = {}
        for dt in range(DT):
            par = dt % 2
            em.wge("vector", "vec", sub_war.get(par, 0))
            last_mul = None
            for tn in range(2):
                last_mul = em.inc("vector", "vec", lambda E_, dt=dt, tn=tn, par=par: E_.tensor_mul(
                    ln_t[par][:, tn * 512:(tn + 1) * 512],
                    X[:, dt * TOK + tn * 512: dt * TOK + (tn + 1) * 512],
                    ps[tn][:, :]))
            em.wge("vector", "vec", last_mul)
            for tn in range(2):
                xn_done = em.inc("vector", "vec", lambda E_, dt=dt, tn=tn, par=par: E_.tensor_sub(
                    XN[:, dt * TOK + tn * 512: dt * TOK + (tn + 1) * 512],
                    ln_t[par][:, tn * 512:(tn + 1) * 512], ps[2 + tn][:, :]))
            sub_war[par] = xn_done
        for i in range(4):
            bank_free[i] = [("vec", xn_done)]
        return xn_done

    def router(xn_done):
        lg_war = [0, 0]
        em.wge("vector", "mm", ffn_state.get("w16_mm", 0) if isinstance(ffn_state, dict) else 0)
        for tt in range(8):
            bank = 4 + tt % 2
            builds = []
            for kt in range(16):
                builds.append(lambda E_, tt=tt, kt=kt, bank=bank: E_.matmul(
                    ps[bank][:, 0:E],
                    XN[:, kt * TOK + tt * 128: kt * TOK + (tt + 1) * 128],
                    rw_sb[:, kt * E:(kt + 1) * E],
                    start=(kt == 0), stop=(kt == 15)))
            done = mm_group(bank, builds,
                            extra_waits=[("vec", xn_done), ("scst", state["ph0_gpd"])])
            em.wge("vector", "mm", done)
            em.wge("vector", "sca", lg_war[0])
            em.wge("vector", "vec", lg_war[1])
            vc0 = em.inc("vector", "vec", lambda E_, bank=bank: E_.tensor_copy(
                lg_sb[:, :], ps[bank][:, 0:E]))
            bank_free[4 + tt % 2] = [("vec", vc0)]
            em.wge("vector", "vec", vc0)
            vmax = em.inc("vector", "vec", lambda E_: E_.max(m8_sb[:, :], lg_sb[:, :]))
            em.wge("vector", "vec", vmax)
            em.inc("vector", "vec", lambda E_: E_.tensor_scalar_mul(nm1_sb[:, :], m8_sb[:, 0:1], -1.0))
            vc1 = em.inc("vector", "vec", lambda E_: E_.tensor_sub(e2a_sb[:, :], m8_sb[:, 1:2], m8_sb[:, 0:1]))
            em.wge("scalar", "vec", vc1)
            em.inc("scalar", "sca", lambda E_: E_.activation(
                sexp_sb[:, :], lg_sb[:, :], AF.Exp, bias=nm1_sb[:, 0:1]))
            sc1 = em.inc("scalar", "sca", lambda E_: E_.activation(
                e2v_sb[:, :], e2a_sb[:, :], AF.Exp))
            lg_war[0] = sc1
            em.wge("vector", "sca", sc1)
            va = em.inc("vector", "vec", lambda E_: E_.tensor_scalar_add(e2v_sb[:, :], e2v_sb[:, :], 1.0))
            em.wge("vector", "vec", va)
            vr = em.inc("vector", "vec", lambda E_: E_.reciprocal(fac_sb[:, :], e2v_sb[:, :]))
            em.wge("vector", "vec", vr)
            em.inc("vector", "vec", lambda E_: E_.tensor_scalar_mul(fac_sb[:, :], fac_sb[:, :], 0.5))
            vm = em.inc("vector", "vec", lambda E_: E_.tensor_scalar(
                msk_sb[:, :], lg_sb[:, :], m8_sb[:, 1:2], None, OP.is_ge))
            em.inc("vector", "vec", lambda E_, tt=tt: E_.tensor_scalar(
                mask1_sb[:, tt * E:(tt + 1) * E], lg_sb[:, :], m8_sb[:, 0:1], None, OP.is_ge))
            em.wge("vector", "vec", vm)
            em.inc("vector", "vec", lambda E_, tt=tt: E_.tensor_copy(
                mask2_sb[:, tt * E:(tt + 1) * E], msk_sb[:, :]))
            em.inc("vector", "vec", lambda E_, tt=tt: E_.scalar_tensor_tensor(
                W16[:, tt * E:(tt + 1) * E], sexp_sb[:, :], fac_sb[:, 0:1], msk_sb[:, :],
                OP.mult, OP.mult))
            lg_war[1] = em.cnt["vec"]
        return em.cnt["vec"]

    ffn_state = {"gslot": 0, "uslot": 0, "dslot": 0, "gp_par": 0, "d_par": 0,
                 "gfree": {}, "ufree": {}, "dfree": {}, "hp_free": 0,
                 "sl_war": {0: 0, 1: 0}, "t1_war": {0: 0, 1: 0}, "x_last": 0}

    def ffn_pass(xn_done):
        st = ffn_state
        w16_done = router(xn_done)

        em.wge_multi("tensor", bank_free["H"])
        em.wge("tensor", "vec", xn_done)
        em.wge("tensor", "gp", state["consts_gp"])
        xw_sem = {0: "xw", 1: "xw1"}
        xwc = {0: em.cnt["xw"], 1: em.cnt["xw1"]}
        for tt in range(8):
            sbuf_half = tt % 2
            for dh in range(2):
                tdone = None
                for dq in range(8):
                    dt = dh * 8 + dq
                    b = lambda E_, tt=tt, dt=dt, dq=dq: E_.transpose(
                        psH[:, dq * 128:(dq + 1) * 128],
                        XN[:, dt * TOK + tt * 128: dt * TOK + (tt + 1) * 128],
                        ident[:, :])
                    tdone = em.inc("tensor", "mm", b)
                em.wge("vector", "mm", tdone)
                em.wge("vector", xw_sem[sbuf_half], xwc[sbuf_half])
                vst = em.inc("vector", "vec", lambda E_, sbuf_half=sbuf_half, dh=dh: E_.tensor_copy(
                    stagT[:, sbuf_half * 2048 + dh * 1024: sbuf_half * 2048 + (dh + 1) * 1024],
                    psH[:, 0:1024]))
                bank_free["H"] = [("vec", vst)]
                em.wge_multi("tensor", bank_free["H"])
            em.wge("sync", "vec", em.cnt["vec"])
            xwc[sbuf_half] = em.inc("sync", xw_sem[sbuf_half],
                                    lambda E_, tt=tt, sbuf_half=sbuf_half: E_.dma_start(
                xnt_hbm.ap()[tt * 128:(tt + 1) * 128, :],
                stagT[:, sbuf_half * 2048:(sbuf_half + 1) * 2048]), 16)
        xw_all = (em.cnt["xw"], em.cnt["xw1"])

        em.wge_multi("tensor", bank_free["H"])
        em.wge("tensor", "vec", w16_done)
        tr_done = None
        for tt in range(8):
            b = lambda E_, tt=tt: E_.transpose(
                psH[0:E, tt * 128:(tt + 1) * 128],
                mask2_sb[:, tt * E:(tt + 1) * E], ident[:, :])
            if tt == 7:
                tr_done = em.inc("tensor", "mm", b)
            else:
                em.run("tensor", b)
        em.wge("vector", "mm", tr_done)
        em.wge("vector", "gp", state["consts_gp"])
        v_pref = em.inc("vector", "vec", lambda E_: E_.tensor_tensor_scan(
            prefT[0:E, :], ones16[0:E, :], psH[0:E, 0:TOK], 0.0, OP.mult, OP.add))
        bank_free["H"] = [("vec", v_pref)]
        em.wge("vector", "vec", v_pref)
        em.wge("vector", "syn", state["ph0_syn"])
        vidf = em.inc("vector", "vec", lambda E_: E_.tensor_copy(
            identF[:, :], ident[0:16, 0:16]))
        em.wge_multi("tensor", bank_free[0] + bank_free[1])
        em.wge("tensor", "vec", vidf)
        for tt in range(8):
            bank = tt % 2
            em.wge_multi("tensor", bank_free[bank])
            trp = em.inc("tensor", "mm", lambda E_, tt=tt, bank=bank: E_.transpose(
                ps[bank][0:128, 0:E], prefT[0:E, tt * 128:(tt + 1) * 128],
                identF[:, :]))
            em.wge("vector", "mm", trp)
            vv = em.inc("vector", "vec", lambda E_, tt=tt, bank=bank: E_.tensor_copy(
                kte[:, tt * E:(tt + 1) * E], ps[bank][:, 0:E]))
            bank_free[bank] = [("vec", vv)]
        kte_done = em.cnt["vec"]

        em.wge("vector", "vec", kte_done)
        em.wge("vector", "syn", state["ph0_syn"])
        wv = (al_dyn["wv0"], al_dyn["wv1"])
        ev = (al_dyn["ev0"], al_dyn["ev1"])
        kv = (al_dyn["kv0"], al_dyn["kv1"])
        for tt in range(8):
            prev = em.cnt["vec"]
            em.wge("vector", "vec", prev)
            vm2 = em.inc("vector", "vec", lambda E_, tt=tt: E_.tensor_sub(
                tmps_a[:, 0:E], mask2_sb[:, tt * E:(tt + 1) * E],
                mask1_sb[:, tt * E:(tt + 1) * E]))
            for r in range(2):
                def mk_ap(tt=tt, r=r):
                    return mask1_sb[:, tt * E:(tt + 1) * E] if r == 0 else tmps_a[:, 0:E]
                if r == 1:
                    em.wge("vector", "vec", vm2)
                for srcv, dstv in ((W16, wv[r]), (erow_sb, ev[r]), (kte, kv[r])):
                    if srcv is W16 or srcv is kte:
                        sap = lambda E_, tt=tt, srcv=srcv: srcv[:, tt * E:(tt + 1) * E]
                    else:
                        sap = lambda E_, srcv=srcv: srcv[:, :]
                    em.wge("vector", "vec", em.cnt["vec"])
                    c1 = em.inc("vector", "vec", lambda E_, sap=sap, mk_ap=mk_ap: E_.tensor_tensor(
                        tmps_b16[:, 0:E], sap(E_), mk_ap(), OP.mult))
                    em.wge("vector", "vec", c1)
                    em.inc("vector", "vec", lambda E_, dstv=dstv, tt=tt: E_.reduce_sum(
                        dstv[:, tt:tt + 1], tmps_b16[:, 0:E], axis=mybir.AxisListType.X))
        vred = em.cnt["vec"]
        em.wge("vector", "vec", vred)
        sc_pre = []
        for r in range(2):
            v1 = em.inc("vector", "vec", lambda E_, r=r: E_.scalar_tensor_tensor(
                sr_f[r][:, :], ev[r][:, :], float(CAP), kv[r][:, :], OP.mult, OP.add))
            em.wge("vector", "vec", v1)
            v2 = em.inc("vector", "vec", lambda E_, r=r: E_.tensor_scalar_add(
                sr_f[r][:, :], sr_f[r][:, :], -1.0))
            em.wge("vector", "vec", v2)
            v3 = em.inc("vector", "vec", lambda E_, r=r: E_.tensor_scalar(
                tmps_b[:, :], kv[r][:, :], float(CAP + 1), None, OP.is_ge))
            em.wge("vector", "vec", v3)
            v4 = em.inc("vector", "vec", lambda E_, r=r: E_.scalar_tensor_tensor(
                sr_f[r][:, :], tmps_b[:, :], 1.0e6, sr_f[r][:, :], OP.mult, OP.add))
            em.wge("vector", "vec", v4)
            v5 = em.inc("vector", "vec", lambda E_, r=r: E_.tensor_copy(
                sr_i[r][:, :], sr_f[r][:, :]))
            em.wge("vector", "vec", v5)
            v6 = em.inc("vector", "vec", lambda E_, r=r: E_.tensor_scalar(
                ia_i[:, :], sr_i[r][:, :], 15, None, OP.bitwise_and))
            em.wge("vector", "vec", v6)
            v7 = em.inc("vector", "vec", lambda E_: E_.tensor_scalar(
                ia_i[:, :], ia_i[:, :], 8, None, OP.logical_shift_left))
            v8 = em.inc("vector", "vec", lambda E_, r=r: E_.tensor_scalar(
                ib_i[:, :], sr_i[r][:, :], 4, None, OP.arith_shift_right))
            em.wge("vector", "vec", max(v7, v8))
            v9 = em.inc("vector", "vec", lambda E_, r=r: E_.tensor_add(
                sr_i[r][:, :], ia_i[:, :], ib_i[:, :]))
            sc_pre.append(v9)
        pairs3 = pairs_sb[:, :].rearrange("p (a b) -> p a b", b=4)
        tok3 = tok1_sb[:, :].rearrange("p (a b) -> p a b", b=1)
        em.wge("vector", "ix", st.get("pair_ix", 0))
        em.inc("vector", "vec", lambda E_: E_.tensor_copy(pairs3[:, :, 0:1], tok3[:, :, :]))
        em.inc("vector", "vec", lambda E_: E_.tensor_copy(pairs3[:, :, 2:3], tok3[:, :, :]))
        em.inc("vector", "vec", lambda E_: E_.tensor_copy(
            pairs3[:, :, 1:2], wv[0][:, :].rearrange("p (a b) -> p a b", b=1)))
        pair_v = em.inc("vector", "vec", lambda E_: E_.tensor_copy(
            pairs3[:, :, 3:4], wv[1][:, :].rearrange("p (a b) -> p a b", b=1)))

        em.wge("gpsimd", "xw", xw_all[0])
        em.wge("gpsimd", "xw1", xw_all[1])
        em.wge("gpsimd", "vec", em.cnt["vec"])
        zst = em.inc("gpsimd", "gp", lambda E_: E_.memset(stagT[:, 0:2048], 0.0))
        st["zst_gp"] = zst
        em.wge("sync", "gp", zst)
        for kz in range(8):
            em.inc("sync", "rz", lambda E_, kz=kz: E_.dma_start(
                routed_tm.ap()[kz * 128:(kz + 1) * 128, :], stagT[:, 0:2048]), 16)
        rz_all = em.cnt["rz"]
        em.wge("gpsimd", "gp", zst)
        em.wge("gpsimd", "wx", st.get("tab_wx", 0))
        tz = em.inc("gpsimd", "tr", lambda E_: E_.dma_start(
            tab_hbm.ap().rearrange("a b -> (a b)").rearrange("(a b) -> a b", a=8),
            stagT[0:8, 0:1024]), 16)

        em.wge("gpsimd", "vec", max(sc_pre + [pair_v]))
        em.wge("gpsimd", "tr", tz)
        for tt in range(8):
            for r in range(2):
                em.inc("gpsimd", "ix", lambda E_, tt=tt, r=r: E_.indirect_dma_start(
                    tab_hbm.ap(), IOA(sr_i[r][:, tt:tt + 1], 0),
                    pairs_sb[:, 4 * tt + 2 * r: 4 * tt + 2 * r + 2], None,
                    bounds_check=16 * CAP - 1, oob_is_err=False), 16)
        ix_all = em.cnt["ix"]
        st["pair_ix"] = ix_all

        em.wge("gpsimd", "ix", ix_all)
        em.wge("gpsimd", "vec", st.get("tab_vec", 0))
        tabw = tab_hbm.ap().rearrange("(q c) v -> q (c v)", q=16)
        for g8 in range(8):
            em.inc("gpsimd", "wx", lambda E_, g8=g8: E_.dma_start(
                tabsb[16 * g8:16 * (g8 + 1), :], tabw[:, :]), 16)
        wx_all = em.cnt["wx"]
        st["tab_wx"] = wx_all
        tabsb3 = tabsb[:, :].rearrange("p (a b) -> p a b", b=2)
        em.wge_multi("tensor", bank_free["G"])
        em.wge("tensor", "vec", kte_done)
        tocc = em.inc("tensor", "mm", lambda E_: E_.transpose(
            psG[0:1, 0:16], prefT[0:16, TOK - 1:TOK], identF[:, :]))
        em.wge("vector", "mm", tocc)
        vo1 = em.inc("vector", "vec", lambda E_: E_.tensor_scalar(
            occrow[0:1, :], psG[0:1, 0:16], float(CAP), None, OP.min))
        bank_free["G"] = [("vec", vo1)]
        em.wge("vector", "vec", vo1)
        vo2 = em.inc("vector", "vec", lambda E_: E_.tensor_copy(occri[0:1, :], occrow[0:1, :]))
        em.wge("vector", "wx", wx_all)
        em.wge("vector", "sc", st.get("idx_free_sc", 0))
        vi1 = em.inc("vector", "vec", lambda E_: E_.tensor_scalar_add(
            idxf[:, :].rearrange("p (a b) -> p a b", b=1), tabsb3[:, :, 0:1], -1.0))
        em.wge("vector", "vec", vi1)
        vi2 = em.inc("vector", "vec", lambda E_: E_.tensor_copy(idxs16[:, :], idxf[:, :]))
        st["idx_vec"] = vi2
        idx_ready = max(vi2, vo2)

        for su in range(2):
            base_g = st["gslot"]
            base_d = st["dslot"]
            first_silu = True
            for fb in range(8):
                gs = base_g + fb
                for ring, d_src, key, freekey, wname in (
                        (gw_ring, d_gw, "gslot", "gfree", "wg"),
                        (uw_ring, d_uw, "uslot", "ufree", "wu")):
                    s = st[key]
                    if s >= 4:
                        em.wge("gpsimd", "mm", st[freekey][s - 4])
                        em.wge("gpsimd", f"{wname}{s % 4}", 16 * (s // 4))
                    em.inc("gpsimd", f"{wname}{s % 4}",
                           lambda E_, su=su, fb=fb, s=s, ring=ring, d_src=d_src:
                           E_.dma_start(ring[:, (s % 4) * 2048:(s % 4 + 1) * 2048], d_src[16 + su, fb]), 16)
                    st[key] = s + 1
                for tn in range(2):
                    p = st["gp_par"] % 2
                    st["gp_par"] += 1
                    pg, pu = ps[p], ps[2 + p]
                    builds = []
                    for kt in range(16):
                        builds.append(lambda E_, gs=gs, kt=kt, tn=tn, pg=pg: E_.matmul(
                            pg[:, :],
                            gw_ring[:, (gs % 4) * 2048 + kt * 128:(gs % 4) * 2048 + (kt + 1) * 128],
                            XN[:, kt * TOK + tn * 512: kt * TOK + (tn + 1) * 512],
                            start=(kt == 0), stop=(kt == 15)))
                    g_done = mm_group(p, builds,
                                      extra_waits=[(f"wg{gs % 4}", 16 * (gs // 4 + 1)), ("vec", xn_done)])
                    if tn == 1:
                        st["gfree"][gs] = g_done
                    builds = []
                    for kt in range(16):
                        builds.append(lambda E_, gs=gs, kt=kt, tn=tn, pu=pu: E_.matmul(
                            pu[:, :],
                            uw_ring[:, (gs % 4) * 2048 + kt * 128:(gs % 4) * 2048 + (kt + 1) * 128],
                            XN[:, kt * TOK + tn * 512: kt * TOK + (tn + 1) * 512],
                            start=(kt == 0), stop=(kt == 15)))
                    u_done = mm_group(2 + p, builds, extra_waits=[(f"wu{gs % 4}", 16 * (gs // 4 + 1))])
                    if tn == 1:
                        st["ufree"][gs] = u_done
                    em.wge("scalar", "mm", g_done)
                    em.wge("scalar", "vec", st["sl_war"][p])
                    sl_done = em.inc("scalar", "sca", lambda E_, p=p, pg=pg: E_.activation(
                        sl_b[p][:, :], pg[:, :], AF.Silu))
                    if first_silu:
                        em.wge("vector", "mm", st["hp_free"])
                        em.wge("vector", "gp", state["consts_gp"])
                        first_silu = False
                    em.wge("vector", "mm", u_done)
                    em.wge("vector", "sca", sl_done)
                    em.wge("vector", "vec", st["t1_war"][p])
                    vt1 = em.inc("vector", "vec", lambda E_, p=p, pu=pu: E_.tensor_mul(
                        t1_b[p][:, :], pu[:, :], sl_b[p][:, :]))
                    st["sl_war"][p] = vt1
                    bank_free[p] = [("sca", sl_done)]
                    bank_free[2 + p] = [("vec", vt1)]
                    em.wge("vector", "vec", vt1)
                    vhp = em.inc("vector", "vec", lambda E_, p=p, fb=fb, tn=tn: E_.tensor_scalar(
                        hp[:, fb * TOK + tn * 512: fb * TOK + (tn + 1) * 512],
                        t1_b[p][:, :], 0.5, None, OP.mult))
                    st["t1_war"][p] = vhp
            hp_done = em.cnt["vec"]
            x_wait = st["x_last"]
            for db in range(16):
                dsq = st["dslot"]
                if dsq >= 4:
                    em.wge("gpsimd", "mm", st["dfree"][dsq - 4])
                    em.wge("gpsimd", f"wd{dsq % 4}", 16 * (dsq // 4))
                em.inc("gpsimd", f"wd{dsq % 4}", lambda E_, su=su, db=db, dsq=dsq:
                       E_.dma_start(dw_ring[:, (dsq % 4) * 1024:(dsq % 4 + 1) * 1024], d_dw[16 + su, db]), 16)
                st["dslot"] = dsq + 1
                for tn in range(2):
                    p = st["d_par"] % 2
                    st["d_par"] += 1
                    pd = ps[4 + p]
                    builds = []
                    for kf in range(8):
                        builds.append(lambda E_, dsq=dsq, kf=kf, tn=tn, pd=pd: E_.matmul(
                            pd[:, :],
                            dw_ring[:, (dsq % 4) * 1024 + kf * 128:(dsq % 4) * 1024 + (kf + 1) * 128],
                            hp[:, kf * TOK + tn * 512: kf * TOK + (tn + 1) * 512],
                            start=(kf == 0), stop=(kf == 7)))
                    d_done = mm_group(4 + p, builds,
                                      extra_waits=[(f"wd{dsq % 4}", 16 * (dsq // 4 + 1)), ("vec", hp_done)])
                    if tn == 1:
                        st["dfree"][dsq] = d_done
                    em.wge("vector", "mm", d_done)
                    em.wge("vector", "vec", x_wait)
                    vc = em.inc("vector", "vec", lambda E_, db=db, tn=tn, pd=pd: E_.tensor_tensor(
                        X[:, db * TOK + tn * 512: db * TOK + (tn + 1) * 512],
                        X[:, db * TOK + tn * 512: db * TOK + (tn + 1) * 512],
                        pd[:, :], OP.add))
                    bank_free[4 + p] = [("vec", vc)]
            st["x_last"] = em.cnt["vec"]
            st["hp_free"] = em.cnt["mm"]

        import os as _os
        if _os.environ.get("MK_NOEXP"):
            st["idx_free_sc"] = em.cnt["sc"]
            st["x_last"] = em.cnt["vec"]
            return st["x_last"]
        wgbc_free = {0: 0, 1: 0}
        pend = None

        def emit_scatter(pe, pbuf, pdgv):
            em.wge("gpsimd", "vec", pdgv)
            em.wge("gpsimd", "rz", rz_all)
            em.wge("gpsimd", "sc", em.cnt["sc"])
            em.inc("gpsimd", "sc", lambda E_, pe=pe, pbuf=pbuf: E_.dma_scatter_add(
                routed_tm.ap(), dg_sb[:, :].rearrange("p (a b) -> p a b", b=2048),
                idxs16[:, 16 * pe:16 * (pe + 1)], CAP, _reg(E_, f"rn{pbuf}"), D), 16)
            st["dg_free"] = em.cnt["sc"]

        for e in range(16):
            buf = e % 2
            base_g = st["gslot"]
            em.wge("gpsimd", "ix", ix_all)
            em.wge("gpsimd", "vec", idx_ready)
            em.wge("gpsimd", "rz", rz_all)
            em.wge("gpsimd", "gp", state["consts_gp"])
            em.wge("gpsimd", "mm", st.get(f"xgd_free{buf}", 0))
            ga_t = em.inc("gpsimd", "ga", lambda E_, e=e, buf=buf: (
                E_.reg_load(_reg(E_, f"rn{buf}"), occri[0:1, e:e + 1]),
                E_.dma_gather(xg_d_b[buf][:, :].rearrange("p (a b) -> p a b", b=256),
                              xnt_hbm.ap(), idxs16[:, 16 * e:16 * (e + 1)], CAP,
                              _reg(E_, f"rn{buf}"), D, transpose=True))[1], 16)
            if pend is not None:
                emit_scatter(*pend)
                pend = None
            em.wge("vector", "wx", wx_all)
            em.wge("vector", "scst", state["ph0_gpd"])
            em.wge("vector", "mm", st.get("wrow_mm", 0))
            vrh = em.inc("vector", "vec", lambda E_, e=e: E_.tensor_tensor(
                rhs16[:, :].rearrange("p (a b) -> p a b", b=16),
                tabsb3[0:16, 16 * e:16 * (e + 1), 1:2].broadcast_to((16, 16, 16)),
                msk16_sb[:, :].rearrange("p (a b) -> p a b", b=16), OP.mult))
            st["tab_vec"] = em.cnt["vec"]
            em.wge_multi("tensor", bank_free["G"])
            em.wge("tensor", "vec", vrh)
            wg_mm = em.inc("tensor", "mm", lambda E_, e=e: E_.matmul(
                psG[:, 0:256], ones16[0:16, 0:128], rhs16[:, :],
                start=True, stop=True))
            st["wrow_mm"] = wg_mm
            em.wge("vector", "mm", wg_mm)
            em.wge("vector", "vec", wgbc_free[buf])
            wg_v = em.inc("vector", "vec", lambda E_, buf=buf: E_.tensor_copy(
                wgbc_b[buf][:, :], psG[:, 0:256]))
            bank_free["G"] = [("vec", wg_v)]
            first_silu = True
            for ft in range(8):
                gs = base_g + ft
                for ring, d_src, key, freekey, wname in (
                        (gw_ring, d_gw, "gslot", "gfree", "wg"),
                        (uw_ring, d_uw, "uslot", "ufree", "wu")):
                    s = st[key]
                    if s >= 4:
                        em.wge("gpsimd", "mm", st[freekey][s - 4])
                        em.wge("gpsimd", f"{wname}{s % 4}", 16 * (s // 4))
                    em.inc("gpsimd", f"{wname}{s % 4}",
                           lambda E_, e=e, ft=ft, s=s, ring=ring, d_src=d_src:
                           E_.dma_start(ring[:, (s % 4) * 2048:(s % 4 + 1) * 2048], d_src[e, ft]), 16)
                    st[key] = s + 1
                p = st["gp_par"] % 2
                st["gp_par"] += 1
                pg, pu = ps[p], ps[2 + p]
                builds = []
                for kt in range(16):
                    builds.append(lambda E_, buf=buf, kt=kt, gs=gs, pg=pg: E_.matmul(
                        pg[:, 0:256],
                        gw_ring[:, (gs % 4) * 2048 + kt * 128:(gs % 4) * 2048 + (kt + 1) * 128],
                        xg_d_b[buf][:, kt * 256:(kt + 1) * 256],
                        start=(kt == 0), stop=(kt == 15)))
                g_done = mm_group(p, builds,
                                  extra_waits=[(f"wg{gs % 4}", 16 * (gs // 4 + 1)), ("ga", ga_t)])
                st["gfree"][gs] = g_done
                builds = []
                for kt in range(16):
                    builds.append(lambda E_, buf=buf, kt=kt, gs=gs, pu=pu: E_.matmul(
                        pu[:, 0:256],
                        uw_ring[:, (gs % 4) * 2048 + kt * 128:(gs % 4) * 2048 + (kt + 1) * 128],
                        xg_d_b[buf][:, kt * 256:(kt + 1) * 256],
                        start=(kt == 0), stop=(kt == 15)))
                u_done = mm_group(2 + p, builds, extra_waits=[(f"wu{gs % 4}", 16 * (gs // 4 + 1))])
                st["ufree"][gs] = u_done
                em.wge("scalar", "mm", g_done)
                em.wge("scalar", "vec", st["sl_war"][p])
                sl_done = em.inc("scalar", "sca", lambda E_, p=p, pg=pg: E_.activation(
                    sl_b[p][:, 0:256], pg[:, 0:256], AF.Silu))
                if first_silu:
                    em.wge("vector", "mm", st["hp_free"])
                    em.wge("vector", "vec", wg_v)
                    first_silu = False
                em.wge("vector", "mm", u_done)
                em.wge("vector", "sca", sl_done)
                em.wge("vector", "vec", st["t1_war"][p])
                vt1 = em.inc("vector", "vec", lambda E_, p=p, pu=pu: E_.tensor_mul(
                    t1_b[p][:, 0:256], pu[:, 0:256], sl_b[p][:, 0:256]))
                st["sl_war"][p] = vt1
                bank_free[p] = [("sca", sl_done)]
                bank_free[2 + p] = [("vec", vt1)]
                em.wge("vector", "vec", vt1)
                vhp = em.inc("vector", "vec", lambda E_, buf=buf, ft=ft, p=p: E_.tensor_mul(
                    hpe[:, ft * 256:(ft + 1) * 256],
                    t1_b[p][:, 0:256], wgbc_b[buf][:, :]))
                st["t1_war"][p] = vhp
            hpe_done = em.cnt["vec"]
            st[f"xgd_free{buf}"] = em.cnt["mm"]
            em.wge("vector", "sc", st.get("dg_free", 0))
            for db in range(16):
                dsq = st["dslot"]
                if dsq >= 4:
                    em.wge("gpsimd", "mm", st["dfree"][dsq - 4])
                    em.wge("gpsimd", f"wd{dsq % 4}", 16 * (dsq // 4))
                em.inc("gpsimd", f"wd{dsq % 4}", lambda E_, e=e, db=db, dsq=dsq:
                       E_.dma_start(dw_ring[:, (dsq % 4) * 1024:(dsq % 4 + 1) * 1024], d_dw[e, db]), 16)
                st["dslot"] = dsq + 1
                for ct in range(2):
                    p = st["d_par"] % 2
                    st["d_par"] += 1
                    pd = ps[4 + p]
                    builds = []
                    for kf in range(8):
                        builds.append(lambda E_, buf=buf, kf=kf, ct=ct, dsq=dsq, pd=pd: E_.matmul(
                            pd[:, 0:128],
                            hpe[:, kf * 256 + ct * 128: kf * 256 + (ct + 1) * 128],
                            dw_ring[:, (dsq % 4) * 1024 + kf * 128:(dsq % 4) * 1024 + (kf + 1) * 128],
                            start=(kf == 0), stop=(kf == 7)))
                    d_done = mm_group(4 + p, builds,
                                      extra_waits=[(f"wd{dsq % 4}", 16 * (dsq // 4 + 1)), ("vec", hpe_done)])
                    if ct == 1:
                        st["dfree"][dsq] = d_done
                    em.wge("vector", "mm", d_done)
                    vdg = em.inc("vector", "vec", lambda E_, db=db, ct=ct, pd=pd: E_.tensor_copy(
                        dg_sb[:, ct * 2048 + db * 128: ct * 2048 + (db + 1) * 128],
                        pd[:, 0:128]))
                    bank_free[4 + p] = [("vec", vdg)]
            dgv = em.cnt["vec"]
            pend = (e, buf, dgv)
            wgbc_free[buf] = hpe_done
            st["hp_free"] = em.cnt["mm"]
        emit_scatter(*pend)
        pend = None
        st["idx_free_sc"] = em.cnt["sc"]

        sc_all = em.cnt["sc"]
        em.wge("sync", "sc", sc_all)
        x_wait = st["x_last"]
        rm_sem = {0: "rm", 1: "rm1"}
        rm_free = {0: 0, 1: 0}
        for dt in range(DT):
            buf = dt % 2
            em.wge("sync", "vec", rm_free[buf])
            rmc = em.inc("sync", rm_sem[buf], lambda E_, dt=dt, buf=buf: E_.dma_start(
                rtd_b[buf][:, :], routed_tm.ap()[:, dt * 128:(dt + 1) * 128],
                transpose=True), 16)
            em.wge("vector", rm_sem[buf], rmc)
            em.wge("vector", "vec", x_wait)
            for tn in range(2):
                vx = em.inc("vector", "vec", lambda E_, dt=dt, tn=tn, buf=buf: E_.tensor_tensor(
                    X[:, dt * TOK + tn * 512: dt * TOK + (tn + 1) * 512],
                    X[:, dt * TOK + tn * 512: dt * TOK + (tn + 1) * 512],
                    rtd_b[buf][:, tn * 512:(tn + 1) * 512], OP.add))
            rm_free[buf] = em.cnt["vec"]
        st["x_last"] = em.cnt["vec"]
        return st["x_last"]

    def xdbl(xn_done):
        for tn in range(2):
            builds = []
            for kt in range(16):
                builds.append(lambda E_, kt=kt, tn=tn: E_.matmul(
                    ps[tn][:, :], xpw_sb[:, kt * 160: kt * 160 + 128],
                    XN[:, kt * TOK + tn * 512: kt * TOK + (tn + 1) * 512],
                    start=(kt == 0), stop=(kt == 15)))
            done = mm_group(tn, builds,
                            extra_waits=[("vec", xn_done), ("scst", state["ph0_gpd"])])
            em.wge("vector", "mm", done)
            vc = em.inc("vector", "vec", lambda E_, tn=tn: E_.tensor_copy(
                delta_rT[:, tn * 512:(tn + 1) * 512], ps[tn][:, :]))
            bank_free[tn] = [("vec", vc)]
        for tn in range(2):
            builds = []
            for kt in range(16):
                builds.append(lambda E_, kt=kt, tn=tn: E_.matmul(
                    ps[2 + tn][0:32, :], xpw_sb[:, kt * 160 + 128: kt * 160 + 160],
                    XN[:, kt * TOK + tn * 512: kt * TOK + (tn + 1) * 512],
                    start=(kt == 0), stop=(kt == 15)))
            done = mm_group(2 + tn, builds)
            em.wge("vector", "mm", done)
            vc = em.inc("vector", "vec", lambda E_, tn=tn: E_.tensor_copy(
                bc_sb[0:32, tn * 512:(tn + 1) * 512], ps[2 + tn][0:32, :]))
            bank_free[2 + tn] = [("vec", vc)]
        bc_v = em.cnt["vec"]
        em.wge("gpsimd", "vec", bc_v)
        bc_w = em.inc("gpsimd", "gpd", lambda E_: E_.dma_start(bc_dram.ap(), bc_sb[0:32, :]), 16)
        return bc_w

    def mamba_pass(which, bc_w, hmz_ready=0):
        dA_free = {}
        dbu_free = {}
        bbc_free = {}
        cbc_free = {}
        em.wge("sync", "gpd", bc_w)
        em.wge("vector", "syn", state["ph0_syn"])
        em.wge("vector", "vec", hmz_ready)
        em.wge("tensor", "scst", state["ph0_gpd"])
        em.wge("scalar", "sca", state["a_sca"])
        hf_last = 0
        prev_vd = 0
        dA_last = 0
        dbu_last = ("gp", 0)
        for dt in range(DT):
            delta_done = None
            for tn in range(2):
                pe = ps[4 + tn]
                d_done = mm_group(4 + tn, [lambda E_, dt=dt, tn=tn, pe=pe: E_.matmul(
                    pe[:, :], dtw_sb[:, dt * 128:(dt + 1) * 128],
                    delta_rT[:, tn * 512:(tn + 1) * 512],
                    start=True, stop=True)])
                em.wge("vector", "mm", d_done)
                em.wge("vector", "vec", em.cnt["vec"])
                em.wge("vector", "sca", em.cnt["sca"])
                vz = em.inc("vector", "vec", lambda E_, dt=dt, pe=pe: E_.tensor_scalar(
                    tmp_m[:, 0:512], pe[:, :], dtb_sb[:, dt:dt + 1], None, OP.add))
                em.wge("vector", "vec", vz)
                vzr = em.inc("vector", "vec", lambda E_: E_.tensor_scalar(
                    tmp_m[:, 512:1024], tmp_m[:, 0:512], 0.0, None, OP.max))
                em.wge("scalar", "vec", vz)
                em.wge("scalar", "sca", em.cnt["sca"])
                caz = em.inc("scalar", "sca", lambda E_: E_.activation(
                    l1p_m[:, :], tmp_m[:, 0:512], AF.Abs))
                em.wge("scalar", "sca", caz)
                ce = em.inc("scalar", "sca", lambda E_: E_.activation(
                    e_m[:, :], l1p_m[:, :], AF.Exp, scale=-1.0))
                em.wge("scalar", "sca", ce)
                em.wge("scalar", "vec", prev_vd)
                sc = em.inc("scalar", "sca", lambda E_: E_.activation(
                    l1p_m[:, :], e_m[:, :], AF.Ln, bias=1.0))
                em.wge("vector", "sca", sc)
                em.wge("vector", "vec", vzr)
                em.wge("vector", "sca", dA_last)
                em.wge("vector", dbu_last[0], dbu_last[1])
                vd = em.inc("vector", "vec", lambda E_, tn=tn: E_.tensor_add(
                    delta_sb[:, tn * 512:(tn + 1) * 512], tmp_m[:, 512:1024], l1p_m[:, :]))
                prev_vd = vd
                em.wge("vector", "vec", vd)
                delta_done = em.inc("vector", "vec", lambda E_, dt=dt, tn=tn: E_.tensor_mul(
                    du_sb[:, tn * 512:(tn + 1) * 512],
                    delta_sb[:, tn * 512:(tn + 1) * 512],
                    XN[:, dt * TOK + tn * 512: dt * TOK + (tn + 1) * 512]))
                bank_free[4 + tn] = [("vec", delta_done), ("sca", ce)]
            for n in range(NST):
                par = n % 2
                em.wge("sync", "gp", bbc_free.get(par, 0))
                em.wge("sync", f"bb{par}", em.cnt[f"bb{par}"])
                bsrc = bass.AP(bc_dram, n * TOK, [[0, 128], [1, TOK]])
                b_ld = em.inc("sync", f"bb{par}", lambda E_, par=par, bsrc=bsrc: E_.dma_start(
                    bbcb[par][:, :], bsrc), 16)
                c_ld = None
                if which == "B":
                    em.wge("sync", "vec", cbc_free.get(par, 0))
                    em.wge("sync", f"cb{par}", em.cnt[f"cb{par}"])
                    csrc = bass.AP(bc_dram, (16 + n) * TOK, [[0, 128], [1, TOK]])
                    c_ld = em.inc("sync", f"cb{par}", lambda E_, par=par, csrc=csrc: E_.dma_start(
                        cbcb[par][:, :], csrc), 16)
                em.wge("scalar", "vec", max(delta_done, dA_free.get(par, 0)))
                sc = em.inc("scalar", "sca", lambda E_, dt=dt, n=n, par=par: E_.activation(
                    dAb[par][:, :], delta_sb[:, :], AF.Exp,
                    scale=A_sb[:, dt * NST + n: dt * NST + n + 1]))
                dA_last = sc
                on_vec = True
                if on_vec:
                    em.wge("vector", f"bb{par}", b_ld)
                    em.wge("vector", "vec", max(delta_done, dbu_free.get(par, 0)))
                    gc = em.inc("vector", "vec", lambda E_, par=par: E_.tensor_mul(
                        dbub[par][:, :], du_sb[:, :], bbcb[par][:, :]))
                    em.wge("gpsimd", "vec", gc)
                    gv = em.inc("gpsimd", "gp", lambda E_: E_.engine_nop())
                    bbc_free[par] = gv
                    dbu_last = ("vec", gc)
                else:
                    em.wge("gpsimd", f"bb{par}", b_ld)
                    em.wge("gpsimd", "vec", max(delta_done, dbu_free.get(par, 0)))
                    gc = em.inc("gpsimd", "gp", lambda E_, par=par: E_.tensor_mul(
                        dbub[par][:, :], du_sb[:, :], bbcb[par][:, :]))
                    bbc_free[par] = gc
                    dbu_last = ("gp", gc)
                em.wge("vector", "sca", sc)
                if on_vec:
                    em.wge("vector", "vec", gc)
                else:
                    em.wge("vector", "gp", gc)
                em.wge("vector", "vec", max(dA_free.get(0, 0), dA_free.get(1, 0)))
                if which == "A":
                    vs = em.inc("vector", "vec", lambda E_, par=par: E_.tensor_tensor_scan(
                        h_sb[:, :], dAb[par][:, :], dbub[par][:, :], 0.0, OP.mult, OP.add))
                    em.wge("vector", "vec", vs)
                    vc = em.inc("vector", "vec", lambda E_, dt=dt, n=n: E_.tensor_copy(
                        hfinA[:, dt * NST + n: dt * NST + n + 1], h_sb[:, TOK - 1:TOK]))
                    hf_last = vc
                else:
                    vs = em.inc("vector", "vec", lambda E_, par=par, dt=dt, n=n: E_.tensor_tensor_scan(
                        h_sb[:, :], dAb[par][:, :], dbub[par][:, :],
                        hmz_sb[:, dt * NST + n: dt * NST + n + 1], OP.mult, OP.add))
                    em.wge("vector", f"cb{par}", c_ld)
                    em.wge("vector", "vec", vs)
                    if n == 0:
                        vc = em.inc("vector", "vec", lambda E_, par=par: E_.tensor_mul(
                            y_sb[:, :], h_sb[:, :], cbcb[par][:, :]))
                    else:
                        vt = em.inc("vector", "vec", lambda E_, par=par: E_.tensor_mul(
                            tmp_bf[:, :], h_sb[:, :], cbcb[par][:, :]))
                        em.wge("vector", "vec", vt)
                        vc = em.inc("vector", "vec", lambda E_: E_.tensor_add(
                            y_sb[:, :], y_sb[:, :], tmp_bf[:, :]))
                    cbc_free[par] = vc
                dA_free[par] = vc
                dbu_free[par] = vc
            if which == "B":
                em.wge("vector", "vec", em.cnt["vec"])
                vst = em.inc("vector", "vec", lambda E_, dt=dt: E_.scalar_tensor_tensor(
                    tmp_m[:, :], XN[:, dt * TOK:(dt + 1) * TOK],
                    dvec_sb[:, dt:dt + 1], y_sb[:, :], OP.mult, OP.add))
                em.wge("vector", "vec", vst)
                em.inc("vector", "vec", lambda E_, dt=dt: E_.tensor_tensor(
                    X[:, dt * TOK:(dt + 1) * TOK], X[:, dt * TOK:(dt + 1) * TOK],
                    tmp_m[:, :], OP.add))
        return em.cnt["vec"] if which == "B" else hf_last

    def exchange(hf_last):
        em.wge("gpsimd", "vec", hf_last)
        c0 = em.cnt["gpd"]

        gpd_sem = em.sem["gpd"]

        def _send(g):
            for arm in g.switch_core_id(n=128):
                c = int(arm.logical)
                g.dma_start(sh_h.ap()[c % 2], hfinA[:, :]).then_inc(gpd_sem, 16)
        em.cnt["gpd"] += 16
        em.run("gpsimd", _send)
        em.wge("gpsimd", "gpd", c0 + 16)

        def _set_flag(g):
            for arm in g.switch_core_id(n=128):
                c = int(arm.logical)
                g.dma_start(sh_f.ap()[c % 2], mg_sb[:, :]).then_inc(gpd_sem, 16)
        em.cnt["gpd"] += 16
        em.run("gpsimd", _set_flag)
        em.wge("gpsimd", "gpd", c0 + 32)

        poll_sem = em.sem["poll"]

        def _poll(g):
            with g.register("rf") as rf, g.register("rt") as rt, g.register("rv") as rv:
                g.reg_mov(rf, 1)
                g.reg_mov(rt, 0)
                with g.While(rf):
                    g.reg_add(rt, rt, 16)
                    for arm in g.switch_core_id(n=128):
                        c = int(arm.logical)
                        g.dma_start(pf_sb[:, :], sh_f.ap()[1 - c % 2]).then_inc(poll_sem, 16)
                    g.wait_ge(poll_sem, rt)
                    g.reg_load(rv, pf_sb[0:1, 0:1])
                    g.reg_alu(rf, rv, MAGIC, OP.bitwise_xor)
        em.run("gpsimd", _poll)

        def _recv(g):
            for arm in g.switch_core_id(n=128):
                c = int(arm.logical)
                g.dma_start(hm_sb[:, :], sh_h.ap()[1 - c % 2]).then_inc(gpd_sem, 16)
        em.cnt["gpd"] += 16
        em.run("gpsimd", _recv)
        em.wge("gpsimd", "gpd", c0 + 48)
        gp_done = em.inc("gpsimd", "gp", lambda E_: E_.engine_nop())
        em.wge("vector", "gp", gp_done)
        em.wge("vector", "syn", state["ph0_syn"])
        return em.inc("vector", "vec", lambda E_: E_.tensor_scalar_mul(
            hmz_sb[:, :], hm_sb[:, :], flagO_sb[:, 0:1]))

    def store(x_ready):
        import os
        tap = os.environ.get("MK_TAP", "X")
        src_t = {"X": X, "XN": XN, "W16": W16, "HP": hp, "HFA": hfinA,
                 "DRT": delta_rT, "BC": bc_sb, "DBG": lg_sb, "DELTA": delta_sb,
                 "DU": du_sb, "H": h_sb, "DA0": dA0, "DA1": dA1,
                 "DBU0": dbu0, "DBU1": dbu1, "Y": y_sb, "HMZ": hmz_sb}.get(tap, X)
        em.wge("gpsimd", "vec", x_ready)
        done = em.cnt["gpd"]
        if tap in ("X", "XN", "HP"):
            nk = 4 if tap != "HP" else 2
            for k in range(nk):
                done = em.inc("gpsimd", "gpd", lambda E_, k=k, src_t=src_t: E_.dma_start(
                    d_out[:, k * 4096:(k + 1) * 4096], src_t[:, k * 4096:(k + 1) * 4096]), 16)
        elif tap == "BC":
            done = em.inc("gpsimd", "gpd", lambda E_: E_.dma_start(
                d_out[0:32, 0:TOK], bc_sb[:, :]), 16)
        else:
            w = src_t.shape[1]
            done = em.inc("gpsimd", "gpd", lambda E_, src_t=src_t, w=w: E_.dma_start(
                d_out[:, 0:w], src_t[:, :]), 16)
        em.wge("gpsimd", "gpd", done)

    x_ready = 0
    ph0()
    em.flush("ph0")
    if "ln1" in phases:
        layernorm(x_ready)
        em.flush("ln1")
    if "rt" in phases:
        xn = layernorm(x_ready)
        router(xn)
        em.flush("rt")
    if "ffn1" in phases:
        xn = layernorm(x_ready)
        x_ready = ffn_pass(xn)
        em.flush("ffn1")
    if "mamba" in phases:
        xn = layernorm(x_ready)
        bc_w = xdbl(xn)
        em.flush("xdbl")
        hf = mamba_pass("A", bc_w)
        em.flush("mambaA")
        hmz_ready = 0
        if "cc" in phases:
            hmz_ready = exchange(hf)
            em.flush("cc")
        x_ready = mamba_pass("B", bc_w, hmz_ready)
        em.flush("mambaB")
    if "ffn2" in phases:
        xn = layernorm(x_ready)
        x_ready = ffn_pass(xn)
        em.flush("ffn2")
    store(x_ready)
    em.flush("store")
    import bass_rust as _br
    from concourse.library_config import all_libraries as _alls, standard as _std
    _mask = {}
    for _lib in _alls:
        for _it in _lib.instructions:
            _mask[_it] = _mask.get(_it, 0) | (1 << _lib.index)
    _br.insert_library_loads(nc, _mask, len(_alls), _std.index)
    mybir.codegen_inst_isa_subclasses(nc)
    return nc


def _prep_inputs(L0_attn_x, L0_attn_A_log, L0_attn_D, L0_attn_x_proj_w,
                 L0_attn_dt_proj_w, L0_attn_dt_proj_b, L0_ffn_router_w,
                 L0_ffn_gate_w, L0_ffn_up_w, L0_ffn_down_w,
                 L0_ffn_shared_gate_w, L0_ffn_shared_up_w, L0_ffn_shared_down_w):
    f = np.float32
    x = np.asarray(L0_attn_x, dtype=f)
    gate_w = np.asarray(L0_ffn_gate_w, dtype=f)
    up_w = np.asarray(L0_ffn_up_w, dtype=f)
    down_w = np.asarray(L0_ffn_down_w, dtype=f)
    sgw = np.asarray(L0_ffn_shared_gate_w, dtype=f)
    suw = np.asarray(L0_ffn_shared_up_w, dtype=f)
    sdw = np.asarray(L0_ffn_shared_down_w, dtype=f)

    def gu_pack(w):
        return w.reshape(16, 16, 128, 8, 128).transpose(0, 3, 2, 1, 4).reshape(16, 8, 128, 2048)

    def gu_pack_shared(w):
        return w.reshape(16, 128, 2, 8, 128).transpose(2, 3, 1, 0, 4).reshape(2, 8, 128, 2048)

    gw = np.ascontiguousarray(np.concatenate([gu_pack(gate_w), gu_pack_shared(sgw)], axis=0))
    uw = np.ascontiguousarray(np.concatenate([gu_pack(up_w), gu_pack_shared(suw)], axis=0))

    def d_pack(w):
        return w.reshape(16, 8, 128, 16, 128).transpose(0, 3, 2, 1, 4).reshape(16, 16, 128, 1024)

    def d_pack_shared(w):
        return w.reshape(2, 8, 128, 16, 128).transpose(0, 3, 2, 1, 4).reshape(2, 16, 128, 1024)

    dw = np.ascontiguousarray(np.concatenate([d_pack(down_w), d_pack_shared(sdw)], axis=0))

    rw = np.ascontiguousarray(np.asarray(L0_ffn_router_w, dtype=f).T.reshape(16, 128, 16))
    xpw = np.ascontiguousarray(np.asarray(L0_attn_x_proj_w, dtype=f).T.reshape(16, 128, 160))
    dtw = np.ascontiguousarray(np.asarray(L0_attn_dt_proj_w, dtype=f).T)
    alog = np.ascontiguousarray(
        np.asarray(L0_attn_A_log, dtype=f).reshape(16, 128, 16).transpose(1, 0, 2).reshape(128, 256))
    dvec = np.ascontiguousarray(np.asarray(L0_attn_D, dtype=f).reshape(16, 128).T)
    dtb = np.ascontiguousarray(np.asarray(L0_attn_dt_proj_b, dtype=f).reshape(16, 128).T)
    consts = np.tile(np.asarray(CONST_VALS, dtype=f), (128, 1))
    magic = np.full((1, 16), MAGIC, np.int32)
    erow = np.tile(np.arange(16, dtype=f), (128, 1))
    tok1 = (np.arange(8, dtype=f)[None, :] * 128 + np.arange(128, dtype=f)[:, None]) + 1.0
    msk16 = (np.arange(256)[None, :] % 16 == np.arange(16)[:, None]).astype(f)

    shared = dict(gw=gw, uw=uw, dw=dw, rw=rw, xpw=xpw, dtw=dtw,
                  alog=alog, dvec=dvec, dtb=dtb, consts=consts, magic=magic,
                  erow=erow, tok1=tok1, msk16=msk16)
    in_maps = []
    for c in range(NCORES):
        b, h = c // 2, c % 2
        xc = np.ascontiguousarray(
            x[b, h * 1024:(h + 1) * 1024, :].T.reshape(16, 128, 1024)
            .transpose(1, 0, 2).reshape(128, 16 * 1024))
        m = dict(shared)
        m.update(xt=xc,
                 flagE=np.full((128, 1), 1.0 - h, dtype=f),
                 flagO=np.full((128, 1), float(h), dtype=f))
        in_maps.append(m)
    return in_maps


_NC_CACHE = {}


def kernel(**inputs):
    import os
    phases = tuple(os.environ.get("MK_PHASES", "ffn1,mamba,cc,ffn2").split(","))
    phases = phases + (os.environ.get("MK_TAP", "X"),)
    in_maps = _prep_inputs(**inputs)
    if _NC_CACHE.get("phases") != phases:
        _NC_CACHE["nc"] = build_program(phases)
        _NC_CACHE["phases"] = phases
    nc = _NC_CACHE["nc"]
    res = run_bass_kernel_spmd(nc, in_maps, core_ids=list(range(NCORES)))
    out = np.empty((4, 2048, 2048), dtype=np.float32)
    for c in range(NCORES):
        b, h = c // 2, c % 2
        oc = np.asarray(res.results[c]["out"])
        out[b, h * 1024:(h + 1) * 1024, :] = (
            oc.reshape(128, 16, 1024).transpose(1, 0, 2).reshape(2048, 1024).T)
    return out
